# version 07
# baseline (speedup 1.0000x reference)
import numpy as np
import ml_dtypes
import concourse.bass as bass
import concourse.mybir as mybir
from concourse.bass_utils import run_bass_kernel_spmd

F32 = mybir.dt.float32
BF16 = mybir.dt.bfloat16
AF = mybir.ActivationFunctionType
ALU = mybir.AluOpType

NT = 1024
NEG = -1.0e30
N_CORES = 8
LAST_CNT = {}
RG = [[0, 1, 2, 3], [4, 5, 6, 7]]


def _runs_for(j):
    return [(512 * j, 512 * (j + 1)), (4096 - 512 * (j + 1), 4096 - 512 * j)]


def _tok_idx(j):
    (a0, a1), (b0, b1) = _runs_for(j)
    return np.concatenate([np.arange(a0, a1), np.arange(b0, b1)])


def _run_owner(g):
    return (g, 0) if g < 4 else (7 - g, 1)


def _slabs(w):
    o, i = w.shape
    wt = np.ascontiguousarray(w.T)
    return np.ascontiguousarray(
        wt.reshape(i // 128, 128, o // 128, 128).transpose(2, 1, 0, 3)
        .reshape(o // 128, 128, (i // 128) * 128))


def _rope_tables(pos):
    inv = 1.0 / (10000.0 ** (np.arange(0, 128, 2, dtype=np.float64) / 128.0))
    ang = pos[None, :].astype(np.float64) * inv[:, None]
    ang2 = np.concatenate([ang, ang], axis=0)
    return np.cos(ang2), np.sin(ang2)


def _hi_lo(a64):
    hi = a64.astype(np.float32)
    return hi, (a64 - hi.astype(np.float64)).astype(np.float32)


def host_prep(inputs):
    g = {}
    wa = np.asarray(inputs["L0_attn_a_w_attn"])
    g["wqk"] = _slabs(wa[:8192])
    g["wv_a"] = _slabs(wa[8192:12288])
    g["wproj"] = _slabs(np.asarray(inputs["L0_attn_a_w_proj"]))
    g["adapter_fm"] = np.ascontiguousarray(
        np.asarray(inputs["L0_attn_a_adapter_wte"]).T.reshape(32, 128, 10)
        .transpose(1, 0, 2).reshape(128, 320))
    g["gating"] = np.asarray(inputs["L0_attn_a_gating"]).reshape(1, 1)
    g["wq_b"] = _slabs(np.asarray(inputs["L0_attn_b_Wq"]))
    g["wk_b"] = _slabs(np.asarray(inputs["L0_attn_b_Wk"]))
    g["wv_b"] = _slabs(np.asarray(inputs["L0_attn_b_Wv"]))
    g["wo_b"] = _slabs(np.asarray(inputs["L0_attn_b_Wo"]))
    g["wg_b"] = _slabs(np.asarray(inputs["L0_attn_b_Wg"]))
    g["win"] = np.ascontiguousarray(
        np.asarray(inputs["L0_attn_b_Win"]).T.reshape(32, 128, 32)
        .transpose(1, 0, 2).reshape(128, 1024))
    g["dt_bias"] = np.asarray(inputs["L0_attn_b_dt_bias"]).reshape(32, 1)
    g["alog"] = np.asarray(inputs["L0_attn_b_A_log_bias"]).reshape(32, 1)
    g["gnorm"] = np.asarray(inputs["L0_attn_b_gnorm_w"]).reshape(128, 1)
    g["lora_a"] = _slabs(np.asarray(inputs["L0_ffn_lora_A"]).reshape(128, 4096))
    Ball = np.asarray(inputs["L0_ffn_lora_B"]).transpose(1, 0, 2).reshape(4096, 128)
    g["lora_b"] = _slabs(Ball)
    g["ffn_w"] = _slabs(np.asarray(inputs["L0_ffn_W"]))
    g["ffn_bias"] = np.ascontiguousarray(
        np.asarray(inputs["L0_ffn_bias"]).reshape(32, 128).T)
    g["lora_w"] = np.asarray(inputs["L0_ffn_lora_weights"]).reshape(8, 1)
    E = np.zeros((8, 128), np.float32)
    for n in range(8):
        E[n, n * 16:(n + 1) * 16] = 2.0
    g["E8"] = E
    g["ident"] = np.eye(128, dtype=np.float32)
    g["ident32s"] = np.tile(np.eye(32, dtype=np.float32), (4, 1))
    g["ones_col"] = np.ones((128, 1), np.float32)
    g["ones_row"] = np.ones((1, 128), np.float32)
    g["ones128"] = np.ones((128, 128), np.float32)
    jj = np.arange(128)[:, None]
    ii = np.arange(128)[None, :]
    g["dmaskT"] = np.where(ii >= jj, 0.0, NEG).astype(np.float32)

    x = np.asarray(inputs["L0_attn_a_x"])
    percore = []
    kk = np.arange(128)[:, None]
    qq = np.arange(512)[None, :]
    for c in range(N_CORES):
        b, j = c // 4, c % 4
        tok = _tok_idx(j)
        d = dict(g)
        d["x_fm"] = np.ascontiguousarray(x[b][tok].T).reshape(32, 128, NT)
        cos64, sin64 = _rope_tables(tok)
        s = 1.0 / np.sqrt(np.float64(128.0))
        for nm, t64 in (("cq", cos64 * s), ("sq", sin64 * s),
                        ("ck", cos64), ("sk", sin64)):
            hi, lo = _hi_lo(t64)
            d[nm + "h"], d[nm + "l"] = hi, lo
        mt = np.full((2, 32, 128, 512), NEG, np.float32)
        for qb in range(2):
            Q0 = 512 * j if qb == 0 else 4096 - 512 * (j + 1)
            for sl in range(32 if qb else 16):
                mt[qb, sl] = np.where(128 * sl + kk <= Q0 + qq, 0.0, NEG)
        d["mask_tbl"] = mt.astype(ml_dtypes.bfloat16)
        selA = np.zeros((1, 8), np.float32)
        selB = np.zeros((1, 8), np.float32)
        if j >= 1:
            selA[0, j - 1] = 1.0
        selB[0, 6 - j] = 1.0
        d["selAB"] = np.concatenate([selA, selB], axis=1)
        percore.append(d)
    return percore


def build_program(stop_after=99, taps=()):
    nc = bass.Bass("TRN2", target_bir_lowering=False,
                   detect_race_conditions=False)

    def din(name, shape, dt=F32):
        return nc.dram_tensor(name, shape, dt, kind="ExternalInput").ap()

    def dstg(name, shape):
        if name in taps:
            return nc.dram_tensor(name, shape, F32, kind="ExternalOutput").ap()
        return nc.dram_tensor(name, shape, F32).ap()

    x_fm = din("x_fm", [32, 128, NT])
    wqk = din("wqk", [64, 128, 4096])
    wv_a = din("wv_a", [32, 128, 4096])
    wproj = din("wproj", [32, 128, 4096])
    adapter_fm = din("adapter_fm", [128, 320])
    gating_in = din("gating", [1, 1])
    wq_b = din("wq_b", [32, 128, 4096])
    wk_b = din("wk_b", [8, 128, 4096])
    wv_b = din("wv_b", [8, 128, 4096])
    wo_b = din("wo_b", [32, 128, 4096])
    wg_b = din("wg_b", [32, 128, 4096])
    win = din("win", [128, 1024])
    dt_bias = din("dt_bias", [32, 1])
    alog = din("alog", [32, 1])
    gnorm_in = din("gnorm", [128, 1])
    lora_a = din("lora_a", [1, 128, 4096])
    lora_b = din("lora_b", [32, 128, 128])
    ffn_w = din("ffn_w", [32, 128, 4096])
    ffn_bias = din("ffn_bias", [128, 32])
    lora_w = din("lora_w", [8, 1])
    E8_in = din("E8", [8, 128])
    ident_in = din("ident", [128, 128])
    ident32s_in = din("ident32s", [128, 32])
    ones_col_in = din("ones_col", [128, 1])
    ones_row_in = din("ones_row", [1, 128])
    ones128_in = din("ones128", [128, 128])
    dmaskT_in = din("dmaskT", [128, 128])
    mask_tbl = din("mask_tbl", [2, 32, 128, 512], BF16)
    selAB_in = din("selAB", [1, 16])
    tab_in = {nm: din(nm, [128, NT]) for nm in
              ("cqh", "cql", "sqh", "sql", "ckh", "ckl", "skh", "skl")}

    st_x1 = dstg("st_x1", [32, 128, NT])
    st_x2 = dstg("st_x2", [32, 128, NT])
    st_x3 = dstg("st_x3", [32, 128, NT])
    st_qa = dstg("st_qa", [32, 128, NT])
    st_ya = dstg("st_ya", [32, 128, NT])
    st_qb = dstg("st_qb", [32, 128, NT])
    st_gate = dstg("st_gate", [32, 128, NT])
    st_yb = dstg("st_yb", [32, 128, NT])
    st_kfm = dstg("st_kfm", [8, 128, NT])
    st_kkm = dstg("st_kkm", [8, 8, 128, 128])
    st_vkm = dstg("st_vkm", [8, 8, 128, 128])
    st_akm = dstg("st_akm", [32, 128, 16])
    st_avk = dstg("st_avk", [32, 10, 128])
    st_ln = dstg("st_ln", [32, 128, NT])
    st_hg = dstg("st_hg", [128, NT])
    ctr_k = dstg("ctr_k", [32, 128, NT])
    ctr_v = dstg("ctr_v", [32, 8, 128, 128])
    k_all = dstg("k_all", [4, 32, 128, NT])
    v_all = dstg("v_all", [4, 32, 8, 128, 128])
    ctr_s = dstg("ctr_s", [2, 32, 128, 128])
    s_all = dstg("s_all", [4, 2, 32, 128, 128])
    ctr_d = dstg("ctr_d", [2, 32])
    d_all = dstg("d_all", [4, 2, 32])
    out_fm = nc.dram_tensor("out_fm", [32, 128, NT], F32,
                            kind="ExternalOutput").ap()

    def sbt(name, shape, dt=F32):
        return nc.alloc_sbuf_tensor(name, shape, dt).ap()

    arena = sbt("arena", [128, 32 * NT])
    abf = arena.bitcast(BF16)
    stg = [sbt("stg0", [128, 2048]), sbt("stg1", [128, 2048])]
    whb = [sbt("whb0", [128, 2048], BF16), sbt("whb1", [128, 2048], BF16)]
    wlb = [sbt("wlb0", [128, 2048], BF16), sbt("wlb1", [128, 2048], BF16)]
    lstg = [sbt("lstg0", [128, 128]), sbt("lstg1", [128, 128])]
    tabs = {nm: sbt("t_" + nm, [128, NT]) for nm in ("cq", "sq", "ck", "sk")}
    bc_m = sbt("bc_m", [128, NT])
    bc_r = sbt("bc_r", [128, NT])
    dmaskT = sbt("dmaskT_s", [128, 128])
    ident = sbt("ident_s", [128, 128])
    ident32s = sbt("ident32s_s", [128, 32])
    ones128 = sbt("ones128_s", [128, 128])
    ones_col = sbt("ones_c", [128, 1])
    ones_row = sbt("ones_r", [1, 128])
    misc = sbt("misc", [128, 2048])
    obuf = sbt("obuf", [128, 2048])
    small = sbt("small", [128, 64])
    adp = sbt("adp", [128, 320])
    holdG = arena[:, 0:NT]
    holdT = arena[:, NT:NT + 512]
    hA = obuf[:, 1024:2048]
    fbias = sbt("fbias", [128, 32])
    selbc = sbt("selbc", [128, 16])

    ps = [nc.alloc_psum_tensor("ps%d" % i, [128, 512], F32).ap()
          for i in range(8)]

    col_gn = small[:, 0:1]
    col_dtb = small[0:32, 1:2]
    col_alog = small[0:32, 2:3]
    val_gat = small[0:1, 3:4]
    col_coef = small[:, 4:5]
    col_lw = small[0:8, 5:6]
    col_rs2 = small[:, 6:7]
    col_gat = small[:, 7:8]
    zcol = small[32:64, 8:9]
    col_eps = small[:, 14:15]
    col_m30 = small[:, 15:16]
    col_m10 = small[:, 16:17]
    row_sel = small[0:1, 24:40]
    onescb = small.bitcast(BF16)[:, 34:35]

    S = {nm: nc.alloc_semaphore(name=nm) for nm in
         ("s_w0", "s_w1", "s_x", "s_x0", "s_x1", "s_mm", "s_a", "s_v",
          "s_st0", "s_st1", "s_st2", "s_st3", "s_sta0", "s_sta1",
          "s_mk0", "s_mk1", "s_d0", "s_d1", "s_g", "s_cc",
          "s_h", "s_l", "s_pw")}
    cnt = {nm: 0 for nm in S}

    def SW(p):
        return S["s_w%d" % (p % 2)]

    def SX(p):
        return S["s_x%d" % (p % 2)]

    def SST(p):
        return S["s_st%d" % p]

    def tot_stores():
        return [(S["s_st%d" % i], cnt["s_st%d" % i]) for i in range(4)] + \
               [(S["s_sta%d" % i], cnt["s_sta%d" % i]) for i in range(2)]
    PH = [0]

    def on():
        PH[0] += 1
        return PH[0] <= stop_after

    if on():
        bx, bv = cnt["s_x"], cnt["s_v"]
        with nc.Block() as blk:
            @blk.sync
            def _(e):
                lo_slots = [misc[:, 0:NT], misc[:, NT:2 * NT],
                            obuf[:, 0:NT], obuf[:, NT:2 * NT]]
                for nm in ("cq", "sq", "ck", "sk"):
                    e.dma_start(tabs[nm], tab_in[nm + "h"]).then_inc(S["s_x"], 16)
                for i, nm in enumerate(("cql", "sql", "ckl", "skl")):
                    e.dma_start(lo_slots[i], tab_in[nm]).then_inc(S["s_x"], 16)
                for dap, src in [
                        (dmaskT, dmaskT_in), (ident, ident_in),
                        (ident32s, ident32s_in),
                        (ones128, ones128_in), (ones_col, ones_col_in),
                        (ones_row, ones_row_in), (col_gn, gnorm_in),
                        (col_dtb, dt_bias), (col_alog, alog),
                        (val_gat, gating_in), (col_lw, lora_w),
                        (row_sel, selAB_in),
                        (fbias, ffn_bias), (adp, adapter_fm)]:
                    e.dma_start(dap, src).then_inc(S["s_x"], 16)

            @blk.vector
            def _(v):
                lo_slots = [misc[:, 0:NT], misc[:, NT:2 * NT],
                            obuf[:, 0:NT], obuf[:, NT:2 * NT]]
                names = ("cq", "sq", "ck", "sk")
                v.wait_ge(S["s_x"], bx + 16 * 22)
                for i in range(4):
                    v.tensor_tensor(tabs[names[i]], tabs[names[i]],
                                    lo_slots[i], op=ALU.add) \
                        .then_inc(S["s_v"], 1)
                v.memset(zcol, 0.0)
                v.memset(onescb, 1.0)
                v.memset(col_eps, 1e-5)
                v.memset(col_m30, -30.0)
                v.memset(col_m10, -10.0)
                v.memset(col_rs2, 2.0).then_inc(S["s_v"], 1)
        cnt["s_x"] += 16 * 22
        cnt["s_v"] += 5

        bmm, bv = cnt["s_mm"], cnt["s_v"]
        bx2 = cnt["s_x"]
        with nc.Block() as blk:
            @blk.sync
            def _(e):
                e.dma_start(misc[0:8, 0:128], E8_in).then_inc(S["s_x"], 16)

            @blk.tensor
            def _(t):
                t.wait_ge(S["s_x"], bx2 + 16)
                t.matmul(ps[0][:, 0:1], misc[0:8, 0:128], col_lw,
                         start=True, stop=True)
                t.matmul(ps[1][:, 0:1], ones_row, val_gat,
                         start=True, stop=True)
                t.matmul(ps[2][:, 0:16], ones_row, row_sel,
                         start=True, stop=True).then_inc(S["s_mm"], 1)

            @blk.scalar
            def _(s):
                s.activation(col_alog, col_alog, AF.Exp)
                s.mul(col_alog, col_alog, -1.0)
                s.activation(col_rs2, col_rs2, AF.Sqrt)
                s.mul(col_rs2, col_rs2, 0.125).then_inc(S["s_a"], 1)

            @blk.vector
            def _(v):
                v.wait_ge(S["s_mm"], bmm + 1)
                v.tensor_copy(col_coef, ps[0][:, 0:1])
                v.tensor_copy(col_gat, ps[1][:, 0:1])
                v.tensor_copy(selbc, ps[2][:, 0:16])
        cnt["s_mm"] += 1
        cnt["s_a"] += 1
        cnt["s_x"] += 16

    def p_ln(src, tap=False):
        xb = [misc[:, 0:NT], obuf[:, 0:NT]]
        sq = [arena[:, 0:NT], arena[:, NT:2 * NT]]
        bx0, bx1 = cnt["s_x0"], cnt["s_x1"]
        ba, bmm = cnt["s_a"], cnt["s_mm"]

        def xw(kc):
            return (SX(kc), (bx0 if kc % 2 == 0 else bx1) + 16 * (kc // 2 + 1))
        with nc.Block() as blk:
            @blk.sync
            def _(e):
                for kc in range(32):
                    if kc >= 2:
                        e.wait_ge(S["s_mm"], bmm + (kc - 1) * 4)
                    e.dma_start(xb[kc % 2], src[kc]).then_inc(SX(kc), 16)

            @blk.tensor
            def _(t):
                for kc in range(32):
                    t.wait_ge(*xw(kc))
                    st, sp = kc == 0, kc == 31
                    t.matmul(ps[4][0:1, :], ones_col, xb[kc % 2][:, 0:512],
                             start=st, stop=sp)
                    t.matmul(ps[5][0:1, :], ones_col, xb[kc % 2][:, 512:1024],
                             start=st, stop=sp)
                    t.wait_ge(S["s_a"], ba + kc + 1)
                    t.matmul(ps[6][0:1, :], ones_col, sq[kc % 2][:, 0:512],
                             start=st, stop=sp)
                    t.matmul(ps[7][0:1, :], ones_col, sq[kc % 2][:, 512:1024],
                             start=st, stop=sp).then_inc(S["s_mm"], 4)

            @blk.scalar
            def _(s):
                for kc in range(32):
                    s.wait_ge(*xw(kc))
                    if kc >= 2:
                        s.wait_ge(S["s_mm"], bmm + (kc - 1) * 4)
                    s.activation(sq[kc % 2], xb[kc % 2], AF.Square) \
                        .then_inc(S["s_a"], 1)
        cnt["s_x0"] += 16 * 16
        cnt["s_x1"] += 16 * 16
        cnt["s_mm"] += 128
        cnt["s_a"] += 32

        r_m = misc[0:1, 0:NT]
        r_r = misc[0:1, NT:2 * NT]
        r_t = obuf[0:1, 1024:2048]
        bv, ba, bmm = cnt["s_v"], cnt["s_a"], cnt["s_mm"]
        with nc.Block() as blk:
            @blk.vector
            def _(v):
                v.tensor_scalar(r_m[0:1, 0:512], ps[4][0:1, :], 1.0 / 4096,
                                None, op0=ALU.mult)
                v.tensor_scalar(r_m[0:1, 512:1024], ps[5][0:1, :], 1.0 / 4096,
                                None, op0=ALU.mult)
                v.tensor_scalar(r_r[0:1, 0:512], ps[6][0:1, :], 1.0 / 4096,
                                None, op0=ALU.mult)
                v.tensor_scalar(r_r[0:1, 512:1024], ps[7][0:1, :], 1.0 / 4096,
                                None, op0=ALU.mult).then_inc(S["s_v"], 1)
                v.wait_ge(S["s_a"], ba + 1)
                v.tensor_tensor(r_r, r_r, r_t, op=ALU.subtract) \
                    .then_inc(S["s_v"], 1)
                v.wait_ge(S["s_a"], ba + 2)
                v.reciprocal(r_r, r_r).then_inc(S["s_v"], 1)

            @blk.scalar
            def _(s):
                s.wait_ge(S["s_v"], bv + 1)
                s.activation(r_t, r_m, AF.Square).then_inc(S["s_a"], 1)
                s.wait_ge(S["s_v"], bv + 2)
                s.activation(r_r, r_r, AF.Sqrt, bias=col_eps[0:1, :]).then_inc(S["s_a"], 1)

            @blk.tensor
            def _(t):
                t.wait_ge(S["s_v"], bv + 3)
                for half in range(2):
                    hs = slice(half * 512, (half + 1) * 512)
                    t.matmul(ps[half * 2], ones_row, r_m[0:1, hs],
                             start=True, stop=True)
                    t.matmul(ps[half * 2 + 1], ones_row, r_r[0:1, hs],
                             start=True, stop=True).then_inc(S["s_mm"], 1)

            @blk.vector
            def _(v):
                for half in range(2):
                    hs = slice(half * 512, (half + 1) * 512)
                    v.wait_ge(S["s_mm"], bmm + half + 1)
                    v.tensor_copy(bc_m[:, hs], ps[half * 2])
                    v.tensor_copy(bc_r[:, hs], ps[half * 2 + 1]) \
                        .then_inc(S["s_v"], 1)
        cnt["s_v"] += 5
        cnt["s_a"] += 2
        cnt["s_mm"] += 2

        bx0, bx1, bv = cnt["s_x0"], cnt["s_x1"], cnt["s_v"]

        def xw2(kc):
            return (SX(kc), (bx0 if kc % 2 == 0 else bx1) + 16 * (kc // 2 + 1))
        with nc.Block() as blk:
            @blk.sync
            def _(e):
                for kc in range(32):
                    if kc >= 2:
                        e.wait_ge(S["s_v"], bv + kc - 1)
                    e.dma_start(xb[kc % 2], src[kc]).then_inc(SX(kc), 16)

            @blk.vector
            def _(v):
                for kc in range(32):
                    v.wait_ge(*xw2(kc))
                    hi = abf[:, kc * 2048:kc * 2048 + 1024]
                    lo = abf[:, kc * 2048 + 1024:(kc + 1) * 2048]
                    v.tensor_tensor(xb[kc % 2], xb[kc % 2], bc_m,
                                    op=ALU.subtract)
                    v.tensor_tensor(xb[kc % 2], xb[kc % 2], bc_r, op=ALU.mult)
                    v.tensor_copy(hi, xb[kc % 2])
                    v.tensor_tensor(lo, xb[kc % 2], hi, op=ALU.subtract) \
                        .then_inc(S["s_v"], 1)
        cnt["s_x0"] += 16 * 16
        cnt["s_x1"] += 16 * 16
        cnt["s_v"] += 32

    def rhs_pair(kc, nt):
        b = kc * 2048 + nt * 512
        return abf[:, b:b + 512], abf[:, b + 1024:b + 1536]

    def p_proj(slabs, n_oc, kind, dstd=None, tabpair=None, resid=None,
               bias=None, lora=None, adapter=None, oc0=0):
        VINC = {"rope": 1, "plain": 1, "vtp": 2, "hA": 1}[kind]
        P = 2 + (1 if adapter else 0) + (2 if kind == "vtp" else 0)
        bw = [cnt["s_w0"], cnt["s_w1"]]
        bxp = [cnt["s_x0"], cnt["s_x1"]]
        bstp = [cnt["s_st0"], cnt["s_st1"]]
        bsta = [cnt["s_sta0"], cnt["s_sta1"]]
        bmk = [cnt["s_mk0"], cnt["s_mk1"]]
        bmm, bv, bg = cnt["s_mm"], cnt["s_v"], cnt["s_g"]
        bh, bl, bpw = cnt["s_h"], cnt["s_l"], cnt["s_pw"]
        rbuf = [misc[:, 0:NT], misc[:, NT:2 * NT]]
        ak_ob = [obuf[:, 1024:1040], obuf[:, 1040:1056]]
        av_ob = [misc[0:10, 512:640], misc[0:10, 640:768]]
        nH = 2 * n_oc

        def upos(u):
            return bmm + (u // 2) * P + (u % 2) + 1

        def adpos(i):
            return bmm + i * P + 3

        def tppos(u):
            return bmm + (u // 2) * P + 2 + (1 if adapter else 0) + (u % 2) + 1

        def dma_w(e, H):
            if bl + H >= 2:
                e.wait_ge(S["s_l"], bl + H - 1)
            if bpw + H >= 2:
                e.wait_ge(S["s_pw"], bpw + H - 1)
            i, h = H // 2, H % 2
            e.dma_start(stg[H % 2],
                        slabs[oc0 + i][:, h * 2048:(h + 1) * 2048]) \
                .then_inc(SW(H), 16)

        def dma_resid(e, i):
            if i >= 2:
                e.wait_ge(S["s_v"], bv + VINC * (2 * i - 2))
            else:
                e.wait_ge(S["s_v"], bv)
            e.dma_start(rbuf[i % 2], resid[oc0 + i]).then_inc(SX(i), 16)

        def split_hi(s, H):
            if bpw + H >= 2:
                s.wait_ge(S["s_pw"], bpw + H - 1)
            s.wait_ge(SW(H), bw[H % 2] + 16 * (H // 2 + 1))
            s.activation(whb[H % 2], stg[H % 2], AF.Copy) \
                .then_inc(S["s_h"], 1)

        def split_lo(v, H):
            v.wait_ge(S["s_h"], bh + H + 1)
            if bpw + H >= 2:
                v.wait_ge(S["s_pw"], bpw + H - 1)
            v.tensor_tensor(wlb[H % 2], stg[H % 2], whb[H % 2],
                            op=ALU.subtract).then_inc(S["s_l"], 1)

        with nc.Block() as blk:
            @blk.sync
            def _(e):
                for H in range(min(4, nH)):
                    dma_w(e, H)
                for i in range(n_oc):
                    if lora is not None:
                        if i >= 2:
                            e.wait_ge(S["s_mm"], bmm + (i - 1) * P)
                        e.dma_start(lstg[i % 2], lora[0][oc0 + i]) \
                            .then_inc(S["s_mk%d" % (i % 2)], 16)
                    if resid is not None and i >= 1:
                        dma_resid(e, i - 1)
                    for H in (2 * i + 4, 2 * i + 5):
                        if H < nH:
                            dma_w(e, H)
                if resid is not None:
                    dma_resid(e, n_oc - 1)

            @blk.tensor
            def _(t):
                for i in range(n_oc):
                    for h in range(2):
                        H = 2 * i + h
                        t.wait_ge(S["s_l"], bl + H + 1)
                        last = None
                        busy = False
                        for nt in range(2):
                            u = 2 * i + nt
                            bank = ps[u % 4]
                            if h == 0 and u >= 4:
                                t.wait_ge(S["s_v"], bv + VINC * (u - 4) + 1)
                            for kc in range(16 * h, 16 * h + 16):
                                c = (kc - 16 * h) * 128
                                wh = whb[H % 2][:, c:c + 128]
                                wl = wlb[H % 2][:, c:c + 128]
                                rh, rl = rhs_pair(kc, nt)
                                t.matmul(bank, wh, rh, start=(kc == 0),
                                         stop=False)
                                t.matmul(bank, wh, rl, start=False, stop=False)
                                sp = (h == 1 and kc == 31 and lora is None)
                                last = t.matmul(bank, wl, rh, start=False,
                                                stop=sp)
                                if sp:
                                    last.then_inc(S["s_mm"], 1)
                                    busy = True
                                else:
                                    busy = False
                        if adapter is not None:
                            if i >= 1 and h == 0:
                                t.wait_ge(S["s_g"], bg + i)
                            for kc in range(16 * h, 16 * h + 16):
                                c = (kc - 16 * h) * 128
                                tile = stg[H % 2][:, c:c + 128]
                                cst, csp = (kc == 0), (kc == 31)
                                if adapter == "ak":
                                    am = t.matmul(ps[6][:, 0:10], tile,
                                                  adp[:, kc * 10:(kc + 1) * 10],
                                                  start=cst, stop=csp)
                                else:
                                    am = t.matmul(ps[7][0:10, 0:128],
                                                  adp[:, kc * 10:(kc + 1) * 10],
                                                  tile, start=cst, stop=csp)
                            busy = False
                            if h == 1:
                                am.then_inc(S["s_mm"], 1)
                                busy = True
                            last = am
                        if busy:
                            t.matmul(ps[5][0:1, 0:1], ones_col[0:1, :],
                                     ones_col[0:1, :], start=True,
                                     stop=True).then_inc(S["s_pw"], 1)
                        else:
                            last.then_inc(S["s_pw"], 1)
                    if lora is not None:
                        t.wait_ge(S["s_mk%d" % (i % 2)],
                                  bmk[i % 2] + 16 * (i // 2 + 1))
                        for nt in range(2):
                            u = 2 * i + nt
                            t.matmul(ps[u % 4], lstg[i % 2],
                                     lora[1][:, nt * 512:(nt + 1) * 512],
                                     start=False, stop=True) \
                                .then_inc(S["s_mm"], 1)
                    if kind == "vtp":
                        for nt in range(2):
                            u = 2 * i + nt
                            t.wait_ge(S["s_v"], bv + VINC * u + 1)
                            if u >= 1:
                                t.wait_ge(S["s_v"], bv + VINC * (u - 1) + 2)
                            for q4 in range(4):
                                tp = t.transpose(
                                    ps[4][:, q4 * 128:(q4 + 1) * 128],
                                    obuf[:, (u % 2) * 512 + q4 * 128:
                                         (u % 2) * 512 + (q4 + 1) * 128],
                                    ident)
                            tp.then_inc(S["s_mm"], 1)

            @blk.scalar
            def _(s):
                for H in range(min(4, nH)):
                    split_hi(s, H)
                for i in range(n_oc):
                    oc = oc0 + i
                    for nt in range(2):
                        u = 2 * i + nt
                        if kind in ("rope", "plain") and dstd is not None:
                            s.wait_ge(S["s_v"], bv + VINC * u + 1)
                            s.dma_start(dstd[oc % 32][:, nt * 512:(nt + 1) * 512],
                                        obuf[:, (u % 2) * 512:(u % 2 + 1) * 512]) \
                                .then_inc(SST(u % 2), 16)
                        elif kind == "vtp":
                            s.wait_ge(S["s_v"], bv + VINC * u + 2)
                            s.dma_start(
                                dstd[oc % 32, nt * 4:(nt + 1) * 4]
                                .transpose([1, 0, 2]),
                                obuf[:, (2 + u % 2) * 512:(3 + u % 2) * 512]
                                .rearrange("p (c d) -> p c d", c=4)) \
                                .then_inc(SST(u % 2), 16)
                    if adapter == "ak":
                        s.wait_ge(S["s_g"], bg + i + 1)
                        s.dma_start(st_akm[oc0 + i - 32][:, 0:16],
                                    ak_ob[i % 2]) \
                            .then_inc(S["s_sta%d" % (i % 2)], 16)
                    elif adapter == "av":
                        s.wait_ge(S["s_g"], bg + i + 1)
                        s.dma_start(st_avk[i], av_ob[i % 2]) \
                            .then_inc(S["s_sta%d" % (i % 2)], 16)
                    for H in (2 * i + 4, 2 * i + 5):
                        if H < nH:
                            split_hi(s, H)

            @blk.vector
            def _(v):
                for H in range(min(4, nH)):
                    split_lo(v, H)
                for i in range(n_oc):
                    oc = oc0 + i
                    for nt in range(2):
                        u = 2 * i + nt
                        bank = ps[u % 4]
                        v.wait_ge(S["s_mm"], upos(u))
                        if u >= 2 and dstd is not None:
                            v.wait_ge(SST(u % 2), bstp[u % 2] + 16 * (u // 2))
                        if kind == "vtp" and u >= 2:
                            v.wait_ge(S["s_mm"], tppos(u - 2))
                        ob = obuf[:, (u % 2) * 512:(u % 2 + 1) * 512]
                        if kind == "rope":
                            cosT, sinT = tabpair
                            tsl = slice(nt * 512, (nt + 1) * 512)
                            tmp_lo = misc[0:64, 1536:2048]
                            tmp_hi = misc[64:128, 1536:2048]
                            v.tensor_tensor(ob[0:64, :], bank[0:64, :],
                                            cosT[0:64, tsl], op=ALU.mult)
                            v.tensor_tensor(tmp_lo, bank[64:128, :],
                                            sinT[0:64, tsl], op=ALU.mult)
                            v.tensor_tensor(ob[0:64, :], ob[0:64, :], tmp_lo,
                                            op=ALU.subtract)
                            v.tensor_tensor(ob[64:128, :], bank[64:128, :],
                                            cosT[64:128, tsl], op=ALU.mult)
                            v.tensor_tensor(tmp_hi, bank[0:64, :],
                                            sinT[64:128, tsl], op=ALU.mult)
                            v.tensor_tensor(ob[64:128, :], ob[64:128, :],
                                            tmp_hi, op=ALU.add) \
                                .then_inc(S["s_v"], 1)
                        elif kind == "plain":
                            if resid is not None:
                                v.wait_ge(SX(i), bxp[i % 2] + 16 * (i // 2 + 1))
                                if bias is not None:
                                    v.scalar_tensor_tensor(
                                        ob, bank, bias(oc),
                                        rbuf[i % 2][:, nt * 512:(nt + 1) * 512],
                                        op0=ALU.add, op1=ALU.add) \
                                        .then_inc(S["s_v"], 1)
                                else:
                                    v.tensor_tensor(
                                        ob, bank,
                                        rbuf[i % 2][:, nt * 512:(nt + 1) * 512],
                                        op=ALU.add).then_inc(S["s_v"], 1)
                            else:
                                v.tensor_copy(ob, bank).then_inc(S["s_v"], 1)
                        elif kind == "vtp":
                            v.tensor_copy(ob, bank).then_inc(S["s_v"], 1)
                            v.wait_ge(S["s_mm"], tppos(u))
                            v.tensor_copy(
                                obuf[:, (2 + u % 2) * 512:(3 + u % 2) * 512],
                                ps[4]).then_inc(S["s_v"], 1)
                        elif kind == "hA":
                            v.tensor_scalar(hA[:, nt * 512:(nt + 1) * 512],
                                            bank, col_coef, None,
                                            op0=ALU.mult).then_inc(S["s_v"], 1)
                    if adapter == "ak":
                        v.wait_ge(S["s_mm"], adpos(i))
                        if i >= 2:
                            v.wait_ge(S["s_sta%d" % (i % 2)],
                                      bsta[i % 2] + 16 * (i // 2))
                        v.memset(ak_ob[i % 2], 0.0)
                        v.tensor_scalar(ak_ob[i % 2][:, 0:10], ps[6][:, 0:10],
                                        col_rs2, None, op0=ALU.mult) \
                            .then_inc(S["s_g"], 1)
                    elif adapter == "av":
                        v.wait_ge(S["s_mm"], adpos(i))
                        if i >= 2:
                            v.wait_ge(S["s_sta%d" % (i % 2)],
                                      bsta[i % 2] + 16 * (i // 2))
                        v.tensor_copy(av_ob[i % 2], ps[7][0:10, 0:128]) \
                            .then_inc(S["s_g"], 1)
                    for H in (2 * i + 4, 2 * i + 5):
                        if H < nH:
                            split_lo(v, H)

        for i in range(n_oc):
            if resid is not None:
                cnt["s_x%d" % (i % 2)] += 16
            if lora is not None:
                cnt["s_mk%d" % (i % 2)] += 16
        cnt["s_w0"] += 16 * n_oc
        cnt["s_w1"] += 16 * n_oc
        cnt["s_h"] += nH
        cnt["s_l"] += nH
        cnt["s_pw"] += nH
        cnt["s_mm"] += n_oc * P
        cnt["s_v"] += VINC * 2 * n_oc
        if kind in ("rope", "plain", "vtp") and dstd is not None:
            cnt["s_st0"] += 16 * n_oc
            cnt["s_st1"] += 16 * n_oc
        if adapter is not None:
            cnt["s_g"] += n_oc
            for i in range(n_oc):
                cnt["s_sta%d" % (i % 2)] += 16

    def barrier_stores():
        tots = tot_stores()
        with nc.Block() as blk:
            @blk.gpsimd
            def _(g):
                for sem, v in tots:
                    g.wait_ge(sem, v)

    def p_load_arena(src, n=32):
        bx0, bx1, bv = cnt["s_x0"], cnt["s_x1"], cnt["s_v"]
        xb = [misc[:, 0:NT], obuf[:, 0:NT]]

        def xw(kc):
            return (SX(kc), (bx0 if kc % 2 == 0 else bx1) + 16 * (kc // 2 + 1))
        with nc.Block() as blk:
            @blk.sync
            def _(e):
                for kc in range(n):
                    if kc >= 2:
                        e.wait_ge(S["s_v"], bv + kc - 1)
                    e.dma_start(xb[kc % 2], src[kc]).then_inc(SX(kc), 16)

            @blk.vector
            def _(v):
                for kc in range(n):
                    v.wait_ge(*xw(kc))
                    hi = abf[:, kc * 2048:kc * 2048 + 1024]
                    lo = abf[:, kc * 2048 + 1024:(kc + 1) * 2048]
                    v.tensor_copy(hi, xb[kc % 2])
                    v.tensor_tensor(lo, xb[kc % 2], hi, op=ALU.subtract) \
                        .then_inc(S["s_v"], 1)
        cnt["s_x0"] += 16 * (n - n // 2)
        cnt["s_x1"] += 16 * (n // 2)
        cnt["s_v"] += n

    def p_dt():
        bw0, bw1 = cnt["s_w0"], cnt["s_w1"]
        bmm, bv = cnt["s_mm"], cnt["s_v"]
        bh, bl, bpw = cnt["s_h"], cnt["s_l"], cnt["s_pw"]
        with nc.Block() as blk:
            @blk.sync
            def _(e):
                e.wait_ge(S["s_l"], bl)
                e.wait_ge(S["s_pw"], bpw)
                e.dma_start(stg[0][:, 0:512], win[:, 0:512]) \
                    .then_inc(S["s_w0"], 16)
                e.dma_start(stg[1][:, 0:512], win[:, 512:1024]) \
                    .then_inc(S["s_w1"], 16)

            @blk.scalar
            def _(s):
                for p in range(2):
                    s.wait_ge(S["s_w%d" % p], (bw0 if p == 0 else bw1) + 16)
                    s.activation(whb[p][:, 0:512], stg[p][:, 0:512], AF.Copy) \
                        .then_inc(S["s_h"], 1)

            @blk.vector
            def _(v):
                for p in range(2):
                    v.wait_ge(S["s_h"], bh + p + 1)
                    v.tensor_tensor(wlb[p][:, 0:512], stg[p][:, 0:512],
                                    whb[p][:, 0:512], op=ALU.subtract) \
                        .then_inc(S["s_l"], 1)

            @blk.tensor
            def _(t):
                t.wait_ge(S["s_l"], bl + 2)
                for nt in range(2):
                    for kc in range(32):
                        p, c = kc // 16, (kc % 16) * 32
                        wh = whb[p][:, c:c + 32]
                        wl = wlb[p][:, c:c + 32]
                        rh, rl = rhs_pair(kc, nt)
                        t.matmul(ps[nt][0:32, :], wh, rh, start=(kc == 0),
                                 stop=False)
                        t.matmul(ps[nt][0:32, :], wh, rl, start=False,
                                 stop=False)
                        mm = t.matmul(ps[nt][0:32, :], wl, rh, start=False,
                                      stop=(kc == 31))
                    mm.then_inc(S["s_mm"], 1)
                t.matmul(ps[4][0:1, 0:1], ones_col[0:1, :], ones_col[0:1, :],
                         start=True, stop=True).then_inc(S["s_pw"], 2)

            @blk.vector
            def _(v):
                v.wait_ge(S["s_mm"], bmm + 2)
                for nt in range(2):
                    v.tensor_copy(holdG[0:32, nt * 512:(nt + 1) * 512],
                                  ps[nt][0:32, :]).then_inc(S["s_v"], 1)
        cnt["s_w0"] += 16
        cnt["s_w1"] += 16
        cnt["s_h"] += 2
        cnt["s_l"] += 2
        cnt["s_pw"] += 2
        cnt["s_mm"] += 2
        cnt["s_v"] += 2

    def p_attention():
        units = [(h, qb) for h in range(32) for qb in range(2)]
        nks = [16 if qb == 0 else 32 for (h, qb) in units]
        ngs = [nk // 8 for nk in nks]
        cum_mm = np.cumsum([0] + [2 * nk + 4 for nk in nks])
        cum_a = np.cumsum([0] + [nk + 1 for nk in nks])
        cum_v = np.cumsum([0] + [2 * nk + 2 for nk in nks])
        ndma = [nk // 4 + 3 for nk in nks]
        bmm, ba, bv, bcc = (cnt["s_mm"], cnt["s_a"], cnt["s_v"], cnt["s_cc"])
        bxp = [cnt["s_x0"], cnt["s_x1"]]
        bstp = [cnt["s_st0"], cnt["s_st1"]]
        bmk = [cnt["s_mk0"], cnt["s_mk1"]]
        bw_a = [cnt["s_w0"], cnt["s_w1"]]
        bh_a, bl_a = cnt["s_h"], cnt["s_l"]
        cum_dma_p = [[0], [0]]
        cum_vdma_p = [[0], [0]]
        for u, nk in enumerate(nks):
            cum_dma_p[u % 2].append(cum_dma_p[u % 2][-1] + ndma[u])
            cum_vdma_p[u % 2].append(cum_vdma_p[u % 2][-1] + nk)

        def mask_pos(g):
            return 1 if g == 0 else 2 * g

        def lo_pos(g, nk):
            return 2 * g + 3 if g <= nk - 2 else 2 * nk
        mk_ord = []
        tot_mk = [0, 0]
        gidx = 0
        for u, nk in enumerate(nks):
            row = []
            for tgi in range(ngs[u]):
                par = gidx % 2
                tot_mk[par] += 1
                row.append((par, tot_mk[par]))
                gidx += 1
            mk_ord.append(row)
        Kb = [arena[:, 0:4096], arena[:, 4096:8192]]
        Vhb = [abf[:, 16384:20480], abf[:, 24576:28672]]
        Vlb = [abf[:, 20480:24576], abf[:, 28672:32768]]
        vstg = arena[:, 27136:31232]
        pth = [abf[:, 52224:52736], abf[:, 53248:53760]]
        ptl = [abf[:, 52736:53248], abf[:, 53760:54272]]
        qbf = [arena[:, 16384:16896], arena[:, 16896:17408]]
        PT = [arena[:, 17408:17920], arena[:, 17920:18432]]
        rlb_sb = arena[:, 18432:18944]
        rlab_sb = arena[:, 18944:19456]
        t2_sb = arena[:, 19456:19968]
        Pa = arena[0:10, 19968:20480]
        akb = [arena[:, 20480:20496], arena[:, 20496:20512]]
        avb = [arena[0:10, 20512:20640], arena[0:10, 20640:20768]]
        lrow = arena[0:1, 20768:21280]
        larow = arena[0:1, 21280:21792]
        mkb = [arena[:, 22016 + i * 2048:22016 + (i + 1) * 2048].bitcast(BF16)
               for i in range(2)]
        gmap = []
        for u in range(len(units)):
            for tgi in range(ngs[u]):
                gmap.append((u, tgi))

        def pe_pos(nk):
            pos = {}
            p = 0
            for key in [("ST", 0), ("ST", 1), "sa"]:
                p += 1
                pos[key] = p
            for g in range(nk):
                p += 1
                pos[("PV", g)] = p
                if g + 2 <= nk - 1:
                    p += 1
                    pos[("ST", g + 2)] = p
            for key in ["la", "rlb", "rlab"]:
                p += 1
                pos[key] = p
            return pos

        with nc.Block() as blk:
            @blk.sync
            def _(e):
                e.wait_ge(S["s_cc"], bcc)
                gi = 0
                for u, (h, qb) in enumerate(units):
                    nk = nks[u]
                    if u >= 2:
                        e.wait_ge(S["s_mm"], bmm + int(cum_mm[u - 1]))
                    if u >= 1:
                        e.wait_ge(S["s_l"], bl_a + u)
                    for R in range(nk // 4):
                        r, sl = _run_owner(R)
                        e.dma_start(Kb[u % 2][:, R * 512:(R + 1) * 512],
                                    k_all[r, h][:, sl * 512:(sl + 1) * 512]) \
                            .then_inc(SX(u), 16)
                        for m in range(4):
                            e.dma_start(
                                vstg[:, R * 512 + m * 128:
                                     R * 512 + (m + 1) * 128],
                                v_all[r, h, sl * 4 + m]) \
                                .then_inc(S["s_w%d" % (u % 2)], 16)
                    e.dma_start(qbf[u % 2],
                                st_qa[h][:, qb * 512:(qb + 1) * 512]) \
                        .then_inc(SX(u), 16)
                    e.dma_start(akb[u % 2], st_akm[h]).then_inc(SX(u), 16)
                    e.dma_start(avb[u % 2], st_avk[h]).then_inc(SX(u), 16)
                    for tgi in range(ngs[u]):
                        par = gi % 2
                        if gi >= 2:
                            pu, pt = gmap[gi - 2]
                            e.wait_ge(S["s_v"],
                                      bv + int(cum_v[pu]) + 16 * pt + 14)
                        e.dma_start(
                            mkb[par].rearrange("p (c n) -> p c n", c=8),
                            mask_tbl[qb, tgi * 8:(tgi + 1) * 8]
                            .transpose([1, 0, 2])) \
                            .then_inc(S["s_mk%d" % par], 16)
                        gi += 1

            @blk.tensor
            def _(t):
                for u, (h, qb) in enumerate(units):
                    nk = nks[u]
                    mmc = bmm + int(cum_mm[u])
                    pos = pe_pos(nk)

                    def st(g):
                        t.matmul(ps[g % 2],
                                 Kb[u % 2][:, g * 128:(g + 1) * 128],
                                 qbf[u % 2], start=True, stop=True) \
                            .then_inc(S["s_mm"], 1)
                    t.wait_ge(SX(u), bxp[u % 2] + 16 * cum_dma_p[u % 2][u // 2 + 1])
                    if u >= 1:
                        t.wait_ge(S["s_v"], bv + int(cum_v[u]))
                    st(0)
                    st(1)
                    t.matmul(ps[3][0:10, :], akb[u % 2][:, 0:10], qbf[u % 2],
                             start=True, stop=True).then_inc(S["s_mm"], 1)
                    t.wait_ge(S["s_h"], bh_a + u + 1)
                    t.wait_ge(S["s_l"], bl_a + u + 1)
                    for g in range(nk):
                        t.wait_ge(S["s_a"], ba + int(cum_a[u]) + g + 1)
                        t.wait_ge(S["s_v"], bv + int(cum_v[u]) + lo_pos(g, nk))
                        vt = slice(g * 128, (g + 1) * 128)
                        t.matmul(ps[4], Vhb[u % 2][:, vt], pth[g % 2],
                                 start=(g == 0), stop=False)
                        t.matmul(ps[4], Vhb[u % 2][:, vt], ptl[g % 2],
                                 start=False, stop=False)
                        t.matmul(ps[4], Vlb[u % 2][:, vt], pth[g % 2],
                                 start=False, stop=(g == nk - 1))
                        t.matmul(ps[5][0:1, :], onescb, pth[g % 2],
                                 start=(g == 0), stop=False)
                        t.matmul(ps[5][0:1, :], onescb, ptl[g % 2],
                                 start=False, stop=(g == nk - 1)) \
                            .then_inc(S["s_mm"], 1)
                        if g + 2 <= nk - 1:
                            st(g + 2)
                    t.wait_ge(S["s_a"], ba + int(cum_a[u]) + nk + 1)
                    t.matmul(ps[6], avb[u % 2], Pa, start=True, stop=True)
                    t.matmul(ps[7][0:1, :], ones_col[0:10, :], Pa,
                             start=True, stop=True).then_inc(S["s_mm"], 1)
                    t.wait_ge(S["s_v"], bv + int(cum_v[u]) + 2 * nk + 1)
                    t.matmul(ps[0], ones_row, lrow, start=True, stop=True) \
                        .then_inc(S["s_mm"], 1)
                    t.matmul(ps[1], ones_row, larow, start=True, stop=True) \
                        .then_inc(S["s_mm"], 1)

            @blk.scalar
            def _(s):
                for u, (h, qb) in enumerate(units):
                    nk = nks[u]
                    mmc = bmm + int(cum_mm[u])
                    pos = pe_pos(nk)
                    s.wait_ge(S["s_w%d" % (u % 2)],
                              bw_a[u % 2] + 16 * cum_vdma_p[u % 2][u // 2 + 1])
                    if u >= 2:
                        s.wait_ge(S["s_mm"], bmm + int(cum_mm[u - 1]))
                    s.activation(Vhb[u % 2][:, 0:nk * 128],
                                 vstg[:, 0:nk * 128], AF.Copy) \
                        .then_inc(S["s_h"], 1)
                    for g in range(nk):
                        s.wait_ge(S["s_v"], bv + int(cum_v[u]) + mask_pos(g))
                        if g < 2:
                            s.wait_ge(S["s_mm"], bmm + int(cum_mm[u]))
                        else:
                            s.wait_ge(S["s_mm"], mmc + pos[("PV", g - 2)])
                        s.activation(PT[g % 2], ps[g % 2], AF.Exp,
                                     bias=col_m30)
                        s.activation(pth[g % 2], PT[g % 2], AF.Copy) \
                            .then_inc(S["s_a"], 1)
                    s.wait_ge(S["s_mm"], mmc + pos["sa"])
                    s.activation(Pa, ps[3][0:10, :], AF.Exp,
                                 bias=col_m10[0:10, :]).then_inc(S["s_a"], 1)
                    s.wait_ge(S["s_v"], bv + int(cum_v[u + 1]))
                    s.dma_start(st_ya[h][:, qb * 512:(qb + 1) * 512],
                                obuf[:, (u % 2) * 512:(u % 2 + 1) * 512]) \
                        .then_inc(SST(u % 2), 16)

            @blk.vector
            def _(v):
                def pt_lo(u, g):
                    v.wait_ge(S["s_a"], ba + int(cum_a[u]) + g + 1)
                    v.tensor_tensor(ptl[g % 2], PT[g % 2], pth[g % 2],
                                    op=ALU.subtract).then_inc(S["s_v"], 1)
                for u, (h, qb) in enumerate(units):
                    nk = nks[u]
                    mmc = bmm + int(cum_mm[u])
                    pos = pe_pos(nk)
                    v.wait_ge(S["s_h"], bh_a + u + 1)
                    if u >= 2:
                        v.wait_ge(S["s_mm"], bmm + int(cum_mm[u - 1]))
                    v.tensor_tensor(Vlb[u % 2][:, 0:nk * 128],
                                    vstg[:, 0:nk * 128],
                                    Vhb[u % 2][:, 0:nk * 128],
                                    op=ALU.subtract).then_inc(S["s_l"], 1)
                    for g in range(nk):
                        v.wait_ge(S["s_mm"], mmc + pos[("ST", g)])
                        par, cnt_mk = mk_ord[u][g // 8]
                        v.wait_ge(S["s_mk%d" % par], bmk[par] + 16 * cnt_mk)
                        v.tensor_tensor(
                            ps[g % 2], ps[g % 2],
                            mkb[par][:, (g % 8) * 512:(g % 8 + 1) * 512],
                            op=ALU.add).then_inc(S["s_v"], 1)
                        if g >= 1:
                            pt_lo(u, g - 1)
                    pt_lo(u, nk - 1)
                    v.wait_ge(S["s_mm"], mmc + pos["la"])
                    v.tensor_copy(lrow, ps[5][0:1, :])
                    v.reciprocal(lrow, lrow)
                    v.tensor_copy(larow, ps[7][0:1, :])
                    v.reciprocal(larow, larow).then_inc(S["s_v"], 1)
                    ob = obuf[:, (u % 2) * 512:(u % 2 + 1) * 512]
                    if u >= 2:
                        v.wait_ge(SST(u % 2), bstp[u % 2] + 16 * (u // 2))
                    v.wait_ge(S["s_mm"], mmc + pos["rlb"])
                    v.tensor_copy(rlb_sb, ps[0])
                    v.tensor_tensor(ob, ps[4], rlb_sb, op=ALU.mult)
                    v.wait_ge(S["s_mm"], mmc + pos["rlab"])
                    v.tensor_copy(rlab_sb, ps[1])
                    v.tensor_tensor(t2_sb, ps[6], rlab_sb, op=ALU.mult)
                    v.scalar_tensor_tensor(ob, t2_sb, col_gat, ob,
                                           op0=ALU.mult, op1=ALU.add) \
                        .then_inc(S["s_v"], 1)

        cnt["s_mm"] += int(cum_mm[-1])
        cnt["s_a"] += int(cum_a[-1])
        cnt["s_v"] += int(cum_v[-1])
        for u in range(len(units)):
            cnt["s_x%d" % (u % 2)] += 16 * ndma[u]
        cnt["s_w0"] += 16 * cum_vdma_p[0][-1]
        cnt["s_w1"] += 16 * cum_vdma_p[1][-1]
        cnt["s_h"] += len(units)
        cnt["s_l"] += len(units)
        cnt["s_mk0"] += 16 * tot_mk[0]
        cnt["s_mk1"] += 16 * tot_mk[1]
        cnt["s_st0"] += 16 * 32
        cnt["s_st1"] += 16 * 32

    def p_kkm():
        bx, bmm, bv = cnt["s_x"], cnt["s_mm"], cnt["s_v"]
        bst4 = [cnt["s_st%d" % i] for i in range(4)]
        with nc.Block() as blk:
            @blk.sync
            def _(e):
                for kv in range(8):
                    e.dma_start(arena[:, 8192 + kv * NT:8192 + (kv + 1) * NT],
                                st_kfm[kv]).then_inc(S["s_x"], 16)

            @blk.tensor
            def _(t):
                t.wait_ge(S["s_x"], bx + 16 * 8)
                for u in range(64):
                    kv, c = u // 8, u % 8
                    if u >= 2:
                        t.wait_ge(S["s_v"], bv + u - 1)
                    t.transpose(ps[4 + u % 2][:, 0:128],
                                arena[:, 8192 + kv * NT + c * 128:
                                      8192 + kv * NT + (c + 1) * 128],
                                ident).then_inc(S["s_mm"], 1)

            @blk.vector
            def _(v):
                for u in range(64):
                    v.wait_ge(S["s_mm"], bmm + u + 1)
                    if u >= 4:
                        v.wait_ge(SST(u % 4), bst4[u % 4] + 16 * (u // 4))
                    v.tensor_copy(obuf[:, (u % 4) * 512:(u % 4) * 512 + 128],
                                  ps[4 + u % 2][:, 0:128]).then_inc(S["s_v"], 1)

            @blk.scalar
            def _(s):
                for u in range(64):
                    kv, c = u // 8, u % 8
                    s.wait_ge(S["s_v"], bv + u + 1)
                    s.dma_start(st_kkm[kv, c],
                                obuf[:, (u % 4) * 512:(u % 4) * 512 + 128]) \
                        .then_inc(SST(u % 4), 16)
        cnt["s_x"] += 16 * 8
        cnt["s_mm"] += 64
        cnt["s_v"] += 64
        for i in range(4):
            cnt["s_st%d" % i] += 16 * 16

    def p_prep():
        ba, bv, bmm = cnt["s_a"], cnt["s_v"], cnt["s_mm"]
        dtr = holdG[0:32, 0:NT]
        gr = holdG[32:64, 0:NT]
        Gr = holdG[64:96, 0:NT]
        t_ln = misc[0:32, 0:NT]
        t_relu = misc[0:32, NT:2 * NT]
        with nc.Block() as blk:
            @blk.vector
            def _(v):
                v.tensor_scalar(dtr, dtr, col_dtb, None, op0=ALU.add) \
                    .then_inc(S["s_v"], 1)
                v.wait_ge(S["s_a"], ba + 3)
                v.tensor_tensor(dtr, t_ln, t_relu, op=ALU.add) \
                    .then_inc(S["s_v"], 1)
                v.tensor_scalar(gr, dtr, col_alog, None, op0=ALU.mult) \
                    .then_inc(S["s_v"], 1)
                v.reduce_sum(small[0:32, 10:11], gr[:, 0:512],
                             axis=mybir.AxisListType.X)
                v.reduce_sum(small[0:32, 11:12], gr[:, 512:1024],
                             axis=mybir.AxisListType.X).then_inc(S["s_v"], 1)
                for c in range(8):
                    v.tensor_tensor_scan(Gr[:, c * 128:(c + 1) * 128],
                                         gr[:, c * 128:(c + 1) * 128],
                                         gr[:, c * 128:(c + 1) * 128],
                                         zcol, op0=ALU.add, op1=ALU.bypass)
                v.memset(small[0:1, 12:13], 0.0).then_inc(S["s_v"], 1)

            @blk.scalar
            def _(s):
                s.wait_ge(S["s_v"], bv + 1)
                s.activation(t_ln, dtr, AF.Abs)
                s.activation(t_ln, t_ln, AF.Exp, scale=-1.0)
                s.activation(t_ln, t_ln, AF.Ln, bias=1.0).then_inc(S["s_a"], 1)
                s.activation(t_relu, dtr, AF.Relu).then_inc(S["s_a"], 2)
                s.wait_ge(S["s_v"], bv + 4)
                s.activation(small[0:32, 10:12], small[0:32, 10:12], AF.Exp) \
                    .then_inc(S["s_a"], 1)
                s.dma_start(ctr_d.transpose([1, 0]), small[0:32, 10:12],
                            allow_slow_non_contiguous=True) \
                    .then_inc(S["s_st0"], 16)

            @blk.tensor
            def _(t):
                t.wait_ge(S["s_v"], bv + 5)
                for c in range(8):
                    t.transpose(ps[4 + c % 2][0:128, 0:32],
                                Gr[:, c * 128:(c + 1) * 128],
                                ident32s[64:96, :]).then_inc(S["s_mm"], 1)
                    t.wait_ge(S["s_v"], bv + 5 + c + 1)
                for c in range(8):
                    t.transpose(ps[6 + c % 2][0:128, 0:32],
                                dtr[:, c * 128:(c + 1) * 128],
                                ident32s[0:32, :]).then_inc(S["s_mm"], 1)
                    t.wait_ge(S["s_v"], bv + 13 + c + 1)

            @blk.vector
            def _(v):
                for c in range(8):
                    v.wait_ge(S["s_mm"], bmm + c + 1)
                    v.tensor_copy(holdT[:, 256 + c * 32:256 + (c + 1) * 32],
                                  ps[4 + c % 2][:, 0:32]).then_inc(S["s_v"], 1)
                for c in range(8):
                    v.wait_ge(S["s_mm"], bmm + 8 + c + 1)
                    v.tensor_copy(holdT[:, c * 32:(c + 1) * 32],
                                  ps[6 + c % 2][:, 0:32]).then_inc(S["s_v"], 1)
        cnt["s_v"] += 5 + 16
        cnt["s_a"] += 4
        cnt["s_mm"] += 16
        cnt["s_st0"] += 16

    def p_sweep1():
        snapA = lambda h: arena[:, 2048 + h * 128:2048 + (h + 1) * 128]
        snapB = lambda h: arena[:, 6144 + h * 128:6144 + (h + 1) * 128]
        stw = lambda h: arena[:, 10240 + h * 128:10240 + (h + 1) * 128]
        colb = [arena[:, 14336 + i * 128:14336 + (i + 1) * 128] for i in range(2)]
        kkmb = [arena[:, 14592 + i * 128:14592 + (i + 1) * 128] for i in range(2)]
        vkmb = [arena[:, 14848 + i * 128:14848 + (i + 1) * 128] for i in range(2)]
        xcb = [arena[:, 15104 + i * 128:15104 + (i + 1) * 128] for i in range(2)]
        kdb = [arena[:, 15360 + i * 128:15360 + (i + 1) * 128] for i in range(2)]
        wcol = [arena[:, 15616 + i:15616 + i + 1] for i in range(2)]
        cdcol = [arena[:, 15620 + i:15620 + i + 1] for i in range(2)]
        units = [(h, r, c) for h in range(32) for r in range(2)
                 for c in range(4)]
        bmm, bv, ba = cnt["s_mm"], cnt["s_v"], cnt["s_a"]
        bxp = [cnt["s_x0"], cnt["s_x1"]]
        bst = cnt["s_st0"]
        with nc.Block() as blk:
            @blk.sync
            def _(e):
                for u, (h, r, c) in enumerate(units):
                    if u >= 2:
                        e.wait_ge(S["s_mm"], bmm + 2 * (u - 1))
                    kv, cl = h // 4, r * 4 + c
                    e.dma_start(kkmb[u % 2], st_kkm[kv, cl]).then_inc(SX(u), 16)
                    e.dma_start(vkmb[u % 2], st_vkm[kv, cl]).then_inc(SX(u), 16)

            @blk.tensor
            def _(t):
                for u, (h, r, c) in enumerate(units):
                    t.wait_ge(S["s_v"], bv + 4 * u + 1)
                    if u >= 2:
                        t.wait_ge(S["s_a"], ba + (u - 2) + 1)
                    t.transpose(ps[0 + u % 2][:, 0:128], colb[u % 2], ident) \
                        .then_inc(S["s_mm"], 1)
                    t.wait_ge(S["s_v"], bv + 4 * u + 3)
                    t.matmul(ps[2 + u % 2][:, 0:128], kdb[u % 2], xcb[u % 2],
                             start=True, stop=True).then_inc(S["s_mm"], 1)

            @blk.scalar
            def _(s):
                bg0 = cnt["s_g"]
                for u, (h, r, c) in enumerate(units):
                    s.wait_ge(S["s_v"], bv + 4 * u + 2)
                    if u >= 2:
                        s.wait_ge(S["s_v"], bv + 4 * (u - 2) + 4)
                    s.activation(wcol[u % 2], misc[:, 1024 + (u % 2):
                                                   1025 + (u % 2)], AF.Exp)
                    s.activation(cdcol[u % 2],
                                 ps[0 + u % 2][:, 127:128], AF.Exp) \
                        .then_inc(S["s_a"], 1)
                    if c == 3:
                        idx = (u + 1) // 4
                        s.wait_ge(S["s_g"], bg0 + idx)
                        s.dma_start(ctr_s[r, h], obuf[:, 1536:1664]) \
                            .then_inc(S["s_st0"], 16)

            @blk.vector
            def _(v):
                for u, (h, r, c) in enumerate(units):
                    cl = r * 4 + c
                    gcol = holdT[:, 256 + cl * 32 + h:256 + cl * 32 + h + 1]
                    dcol = holdT[:, cl * 32 + h:cl * 32 + h + 1]
                    v.tensor_scalar(colb[u % 2], ones128, gcol, None,
                                    op0=ALU.mult).then_inc(S["s_v"], 1)
                    v.wait_ge(S["s_mm"], bmm + 2 * u + 1)
                    v.tensor_scalar(misc[:, 1024 + (u % 2):1025 + (u % 2)],
                                    ps[0 + u % 2][:, 127:128], gcol, None,
                                    op0=ALU.subtract).then_inc(S["s_v"], 1)
                    v.wait_ge(SX(u), bxp[u % 2] + 16 * 2 * (u // 2 + 1))
                    v.wait_ge(S["s_a"], ba + u + 1)
                    v.tensor_scalar(xcb[u % 2], vkmb[u % 2], dcol, None,
                                    op0=ALU.mult)
                    v.tensor_scalar(kdb[u % 2], kkmb[u % 2], wcol[u % 2], None,
                                    op0=ALU.mult).then_inc(S["s_v"], 1)
                    v.wait_ge(S["s_mm"], bmm + 2 * u + 2)
                    if c == 0:
                        v.tensor_copy(stw(h), ps[2 + u % 2][:, 0:128]) \
                            .then_inc(S["s_v"], 1)
                    else:
                        v.scalar_tensor_tensor(stw(h), stw(h), cdcol[u % 2],
                                               ps[2 + u % 2][:, 0:128],
                                               op0=ALU.mult, op1=ALU.add) \
                            .then_inc(S["s_v"], 1)
                    if c == 3:
                        k = (u + 1) // 4
                        if k >= 2:
                            v.wait_ge(S["s_st0"], bst + 16 * (k - 1))
                        v.tensor_copy(obuf[:, 1536:1664], stw(h)) \
                            .then_inc(S["s_g"], 1)


        cnt["s_x0"] += 16 * len(units)
        cnt["s_x1"] += 16 * len(units)
        cnt["s_mm"] += 2 * len(units)
        cnt["s_v"] += 4 * len(units)
        cnt["s_a"] += len(units)
        cnt["s_g"] += 64
        cnt["s_st0"] += 16 * 64

        tots1 = tot_stores()
        bcc = cnt["s_cc"]
        with nc.Block() as blk:
            @blk.gpsimd
            def _(g):
                for sem, vv in tots1:
                    g.wait_ge(sem, vv)
                g.collective_compute("AllGather", ALU.bypass,
                                     replica_groups=RG, ins=[ctr_s.opt()],
                                     outs=[s_all.opt()]).then_inc(S["s_cc"], 1)
                g.collective_compute("AllGather", ALU.bypass,
                                     replica_groups=RG, ins=[ctr_d.opt()],
                                     outs=[d_all.opt()]).then_inc(S["s_cc"], 1)
        cnt["s_cc"] += 2

    def p_prefix():
        snapA = lambda h: arena[:, 2048 + h * 128:2048 + (h + 1) * 128]
        snapB = lambda h: arena[:, 6144 + h * 128:6144 + (h + 1) * 128]
        Tz = lambda g: arena[:, 10240 + g * 128:10240 + (g + 1) * 128]
        sbuf_s = [arena[:, 11264 + i * 128:11264 + (i + 1) * 128]
                  for i in range(2)]
        rowD = misc[0:1, 0:256]
        Dbc = arena[:, 11520:11776]
        bx, bmm, bv, bcc = cnt["s_x"], cnt["s_mm"], cnt["s_v"], cnt["s_cc"]
        bd = [cnt["s_d0"], cnt["s_d1"]]
        with nc.Block() as blk:
            @blk.sync
            def _(e):
                e.wait_ge(S["s_cc"], bcc)
                for g in range(8):
                    r, sl = _run_owner(g)
                    e.dma_start(rowD[0:1, g * 32:(g + 1) * 32], d_all[r, sl]) \
                        .then_inc(S["s_x"], 16)
                for h in range(32):
                    for g in range(7):
                        k = h * 7 + g
                        r, sl = _run_owner(g)
                        if k >= 2:
                            e.wait_ge(S["s_v"], bv + 1 + k - 1)
                        e.dma_start(sbuf_s[k % 2], s_all[r, sl, h]) \
                            .then_inc(S["s_d%d" % (k % 2)], 16)

            @blk.tensor
            def _(t):
                t.wait_ge(S["s_x"], bx + 16 * 8)
                t.matmul(ps[0][:, 0:256], ones_row, rowD,
                         start=True, stop=True).then_inc(S["s_mm"], 1)

            @blk.vector
            def _(v):
                v.wait_ge(S["s_mm"], bmm + 1)
                v.tensor_copy(Dbc, ps[0][:, 0:256]).then_inc(S["s_v"], 1)
                for h in range(32):
                    for g in range(7):
                        k = h * 7 + g
                        v.wait_ge(S["s_d%d" % (k % 2)],
                                  bd[k % 2] + 16 * (k // 2 + 1))
                        if g == 0:
                            v.tensor_copy(Tz(0), sbuf_s[k % 2])
                        else:
                            v.scalar_tensor_tensor(
                                Tz(g), Tz(g - 1), Dbc[:, g * 32 + h:
                                                      g * 32 + h + 1],
                                sbuf_s[k % 2], op0=ALU.mult, op1=ALU.add)
                        v.memset(small[0:1, 13:14], 0.0) \
                            .then_inc(S["s_v"], 1)
                    v.memset(snapA(h), 0.0)
                    v.memset(snapB(h), 0.0)
                    for g in range(7):
                        v.scalar_tensor_tensor(snapA(h), Tz(g),
                                               selbc[:, g:g + 1], snapA(h),
                                               op0=ALU.mult, op1=ALU.add)
                        v.scalar_tensor_tensor(snapB(h), Tz(g),
                                               selbc[:, 8 + g:9 + g], snapB(h),
                                               op0=ALU.mult, op1=ALU.add)
        cnt["s_x"] += 16 * 8
        cnt["s_d0"] += 16 * 112
        cnt["s_d1"] += 16 * 112
        cnt["s_mm"] += 1
        cnt["s_v"] += 1 + 224

    def p_sweep2():
        snapA = lambda h: arena[:, 2048 + h * 128:2048 + (h + 1) * 128]
        snapB = lambda h: arena[:, 6144 + h * 128:6144 + (h + 1) * 128]
        Z = 12288
        BW = 2048
        colb = [arena[:, Z + i * BW:Z + i * BW + 128] for i in range(2)]
        dcT = [arena[:, Z + i * BW + 128:Z + i * BW + 256] for i in range(2)]
        exG = [arena[:, Z + i * BW + 256:Z + i * BW + 384] for i in range(2)]
        SDT = [arena[:, Z + i * BW + 384:Z + i * BW + 512] for i in range(2)]
        xcb = [arena[:, Z + i * BW + 512:Z + i * BW + 640] for i in range(2)]
        kdb = [arena[:, Z + i * BW + 640:Z + i * BW + 768] for i in range(2)]
        qsc = [arena[:, Z + i * BW + 768:Z + i * BW + 896] for i in range(2)]
        ysq = [arena[:, Z + i * BW + 896:Z + i * BW + 1024] for i in range(2)]
        rbs = [arena[:, Z + i * BW + 1024:Z + i * BW + 1152] for i in range(2)]
        ynb = [arena[:, Z + i * BW + 1152:Z + i * BW + 1280] for i in range(2)]
        gsb = [arena[:, Z + i * BW + 1280:Z + i * BW + 1408] for i in range(2)]
        rro = [arena[0:1, Z + i * BW + 1412:Z + i * BW + 1540]
               for i in range(2)]
        Z2 = Z + 2 * BW
        qb_ = [arena[:, Z2 + i * 128:Z2 + (i + 1) * 128] for i in range(2)]
        gt_ = [arena[:, Z2 + 256 + i * 128:Z2 + 256 + (i + 1) * 128]
               for i in range(2)]
        kf_ = [arena[:, Z2 + 512 + i * 128:Z2 + 512 + (i + 1) * 128]
               for i in range(2)]
        kk_ = [arena[:, Z2 + 768 + i * 128:Z2 + 768 + (i + 1) * 128]
               for i in range(2)]
        vk_ = [arena[:, Z2 + 1024 + i * 128:Z2 + 1024 + (i + 1) * 128]
               for i in range(2)]
        units = [(h, r, c) for h in range(32) for r in range(2)
                 for c in range(4)]
        bmm, bv, ba = cnt["s_mm"], cnt["s_v"], cnt["s_a"]
        bxp = [cnt["s_x0"], cnt["s_x1"]]
        bst4 = [cnt["s_st%d" % i] for i in range(4)]
        with nc.Block() as blk:
            @blk.sync
            def _(e):
                for u, (h, r, c) in enumerate(units):
                    if u >= 2:
                        e.wait_ge(S["s_mm"], bmm + 6 * (u - 1))
                    kv, cl = h // 4, r * 4 + c
                    tsl = slice(cl * 128, (cl + 1) * 128)
                    e.dma_start(qb_[u % 2], st_qb[h][:, tsl]).then_inc(SX(u), 16)
                    e.dma_start(gt_[u % 2], st_gate[h][:, tsl]).then_inc(SX(u), 16)
                    e.dma_start(kf_[u % 2], st_kfm[kv][:, tsl]).then_inc(SX(u), 16)
                    e.dma_start(kk_[u % 2], st_kkm[kv, cl]).then_inc(SX(u), 16)
                    e.dma_start(vk_[u % 2], st_vkm[kv, cl]).then_inc(SX(u), 16)

            @blk.tensor
            def _(t):
                for u, (h, r, c) in enumerate(units):
                    m0 = bmm + 6 * u
                    v0 = bv + 6 * u
                    a0 = ba + 4 * u
                    t.wait_ge(S["s_v"], v0 + 1)
                    if u >= 1:
                        t.wait_ge(S["s_a"], ba + 4 * (u - 1) + 4)
                    t.transpose(ps[0][:, 0:128], colb[u % 2], ident) \
                        .then_inc(S["s_mm"], 1)
                    t.wait_ge(SX(u), bxp[u % 2] + 16 * 5 * (u // 2 + 1))
                    t.matmul(ps[1][:, 0:128], kf_[u % 2], qb_[u % 2],
                             start=True, stop=True).then_inc(S["s_mm"], 1)
                    t.wait_ge(S["s_v"], v0 + 3)
                    st_tile = snapA(h) if r == 0 else snapB(h)
                    t.matmul(ps[2][:, 0:128], xcb[u % 2], SDT[u % 2],
                             start=True, stop=False)
                    t.matmul(ps[2][:, 0:128], st_tile, qsc[u % 2],
                             start=False, stop=True).then_inc(S["s_mm"], 1)
                    t.matmul(ps[3][:, 0:128], kdb[u % 2], xcb[u % 2],
                             start=True, stop=True).then_inc(S["s_mm"], 1)
                    t.wait_ge(S["s_a"], a0 + 2)
                    t.matmul(ps[4][0:1, 0:128], ones_col, ysq[u % 2],
                             start=True, stop=True).then_inc(S["s_mm"], 1)
                    t.wait_ge(S["s_v"], v0 + 5)
                    t.matmul(ps[5][:, 0:128], ones_row, rro[u % 2],
                             start=True, stop=True).then_inc(S["s_mm"], 1)

            @blk.scalar
            def _(s):
                for u, (h, r, c) in enumerate(units):
                    m0 = bmm + 6 * u
                    v0 = bv + 6 * u
                    a0 = ba + 4 * u
                    s.wait_ge(S["s_v"], v0 + 2)
                    if u >= 2:
                        s.wait_ge(S["s_v"], bv + 6 * (u - 2) + 3)
                    s.activation(dcT[u % 2], misc[:, 1152 + (u % 2) * 128:
                                                   1152 + (u % 2 + 1) * 128],
                                 AF.Exp)
                    s.activation(exG[u % 2], ps[0][:, 0:128], AF.Exp)
                    s.activation(misc[:, 1024 + (u % 2):1025 + (u % 2)],
                                 misc[:, 1026 + (u % 2):1027 + (u % 2)],
                                 AF.Exp).then_inc(S["s_a"], 1)
                    s.wait_ge(S["s_mm"], m0 + 3)
                    s.activation(ysq[u % 2], ps[2][:, 0:128], AF.Square) \
                        .then_inc(S["s_a"], 1)
                    s.wait_ge(S["s_mm"], m0 + 5)
                    s.activation(rro[u % 2], ps[4][0:1, 0:128], AF.Sqrt,
                                 scale=1.0 / 128.0, bias=col_eps[0:1, :]) \
                        .then_inc(S["s_a"], 1)
                    s.wait_ge(SX(u), bxp[u % 2] + 16 * 5 * (u // 2 + 1))
                    s.activation(gsb[u % 2], gt_[u % 2], AF.Silu) \
                        .then_inc(S["s_a"], 1)
                    s.wait_ge(S["s_v"], bv + 6 * (u + 1))
                    s.dma_start(st_yb[h][:, (r * 4 + c) * 128:
                                         (r * 4 + c + 1) * 128],
                                obuf[:, (u % 4) * 512:(u % 4) * 512 + 128]) \
                        .then_inc(SST(u % 4), 16)

            @blk.vector
            def _(v):
                for u, (h, r, c) in enumerate(units):
                    m0 = bmm + 6 * u
                    v0 = bv + 6 * u
                    a0 = ba + 4 * u
                    cl = r * 4 + c
                    gcol = holdT[:, 256 + cl * 32 + h:256 + cl * 32 + h + 1]
                    dcol = holdT[:, cl * 32 + h:cl * 32 + h + 1]
                    if u >= 4:
                        v.wait_ge(SST(u % 4), bst4[u % 4] + 16 * (u // 4))
                    v.tensor_scalar(colb[u % 2], ones128, gcol, None,
                                    op0=ALU.mult).then_inc(S["s_v"], 1)
                    v.wait_ge(S["s_mm"], m0 + 1)
                    v.scalar_tensor_tensor(
                        misc[:, 1152 + (u % 2) * 128:1152 + (u % 2 + 1) * 128],
                        ps[0][:, 0:128], gcol, dmaskT,
                        op0=ALU.subtract, op1=ALU.add)
                    v.tensor_scalar(misc[:, 1026 + (u % 2):1027 + (u % 2)],
                                    ps[0][:, 127:128], gcol, None,
                                    op0=ALU.subtract).then_inc(S["s_v"], 1)
                    v.wait_ge(S["s_mm"], m0 + 2)
                    v.wait_ge(S["s_a"], a0 + 1)
                    v.tensor_tensor(SDT[u % 2], ps[1][:, 0:128], dcT[u % 2],
                                    op=ALU.mult)
                    v.tensor_scalar(xcb[u % 2], vk_[u % 2], dcol, None,
                                    op0=ALU.mult)
                    v.tensor_scalar(kdb[u % 2], kk_[u % 2],
                                    misc[:, 1024 + (u % 2):1025 + (u % 2)],
                                    None, op0=ALU.mult)
                    v.tensor_tensor(qsc[u % 2], qb_[u % 2], exG[u % 2],
                                    op=ALU.mult).then_inc(S["s_v"], 1)
                    v.wait_ge(S["s_mm"], m0 + 4)
                    st_tile = snapA(h) if r == 0 else snapB(h)
                    v.scalar_tensor_tensor(st_tile, st_tile,
                                           exG[u % 2][:, 127:128],
                                           ps[3][:, 0:128],
                                           op0=ALU.mult, op1=ALU.add) \
                        .then_inc(S["s_v"], 1)
                    v.wait_ge(S["s_a"], a0 + 3)
                    v.reciprocal(rro[u % 2], rro[u % 2]).then_inc(S["s_v"], 1)
                    v.wait_ge(S["s_mm"], m0 + 6)
                    v.tensor_copy(rbs[u % 2], ps[5][:, 0:128])
                    v.scalar_tensor_tensor(ynb[u % 2], ps[2][:, 0:128],
                                           col_gn, rbs[u % 2],
                                           op0=ALU.mult, op1=ALU.mult)
                    v.wait_ge(S["s_a"], a0 + 4)
                    v.tensor_tensor(obuf[:, (u % 4) * 512:(u % 4) * 512 + 128],
                                    ynb[u % 2], gsb[u % 2], op=ALU.mult) \
                        .then_inc(S["s_v"], 1)

        cnt["s_x0"] += 16 * 5 * (len(units) // 2)
        cnt["s_x1"] += 16 * 5 * (len(units) // 2)
        cnt["s_mm"] += 6 * len(units)
        cnt["s_a"] += 4 * len(units)
        cnt["s_v"] += 6 * len(units)
        for i in range(4):
            cnt["s_st%d" % i] += 16 * (len(units) // 4)

    if on():
      with nc.named_scope("ln0"):
        p_ln(x_fm, tap=("st_ln" in taps))
    if on():
      with nc.named_scope("proj_ka"):
        p_proj(wqk, 32, "rope", dstd=ctr_k, tabpair=(tabs["ck"], tabs["sk"]),
               adapter="ak", oc0=32)
    if on():
      with nc.named_scope("proj_va"):
        p_proj(wv_a, 32, "vtp", dstd=ctr_v, adapter="av")
    if on():
      with nc.named_scope("proj_qa"):
        tots0 = tot_stores()
        p_proj(wqk, 32, "rope", dstd=st_qa, tabpair=(tabs["cq"], tabs["sq"]))
        with nc.Block() as blk:
            @blk.gpsimd
            def _(g):
                for sem, vv in tots0:
                    g.wait_ge(sem, vv)
                g.collective_compute("AllGather", ALU.bypass,
                                     replica_groups=RG, ins=[ctr_k.opt()],
                                     outs=[k_all.opt()]).then_inc(S["s_cc"], 1)
                g.collective_compute("AllGather", ALU.bypass,
                                     replica_groups=RG, ins=[ctr_v.opt()],
                                     outs=[v_all.opt()]).then_inc(S["s_cc"], 1)
        cnt["s_cc"] += 2
        barrier_stores()
    if on():
      with nc.named_scope("attention"):
        p_attention()
        barrier_stores()
    if on():
      with nc.named_scope("proj_o_a"):
        p_load_arena(st_ya)
        p_proj(wproj, 32, "plain", dstd=st_x1, resid=x_fm)
        barrier_stores()
    if on():
      with nc.named_scope("ln1"):
        p_ln(st_x1)
    if on():
      with nc.named_scope("ffn1"):
        p_proj(lora_a, 1, "hA")
        p_proj(ffn_w, 32, "plain", dstd=st_x2, resid=st_x1,
               bias=lambda oc: fbias[:, oc:oc + 1], lora=(lora_b, hA))
        barrier_stores()
    if on():
      with nc.named_scope("ln2"):
        p_ln(st_x2)
    if on():
      with nc.named_scope("proj_qb"):
        p_proj(wq_b, 32, "rope", dstd=st_qb, tabpair=(tabs["ck"], tabs["sk"]))
    if on():
      with nc.named_scope("proj_kb"):
        p_proj(wk_b, 8, "rope", dstd=st_kfm, tabpair=(tabs["ck"], tabs["sk"]))
    if on():
      with nc.named_scope("proj_vb"):
        p_proj(wv_b, 8, "vtp", dstd=st_vkm)
    if on():
      with nc.named_scope("proj_gb"):
        p_proj(wg_b, 32, "plain", dstd=st_gate)
    if on():
      with nc.named_scope("p_dt"):
        p_dt()
        barrier_stores()
    if on():
      with nc.named_scope("p_kkm"):
        p_kkm()
        barrier_stores()
    if on():
      with nc.named_scope("p_prep"):
        p_prep()
        if "st_hg" in taps:
            bst3 = cnt["s_st"]
            with nc.Block() as blk:
                @blk.scalar
                def _(s):
                    s.dma_start(st_hg, holdG).then_inc(S["s_st"], 16)
                    s.wait_ge(S["s_st"], bst3 + 16)
            cnt["s_st"] += 16
        barrier_stores()
    if on():
      with nc.named_scope("sweep1"):
        p_sweep1()
    if on():
      with nc.named_scope("prefix"):
        p_prefix()
    if on():
      with nc.named_scope("sweep2"):
        p_sweep2()
        barrier_stores()
    if on():
      with nc.named_scope("proj_o_b"):
        p_load_arena(st_yb)
        p_proj(wo_b, 32, "plain", dstd=st_x3, resid=st_x2)
        barrier_stores()
    if on():
      with nc.named_scope("ln3"):
        p_ln(st_x3)
    if on():
      with nc.named_scope("ffn2"):
        p_proj(lora_a, 1, "hA")
        p_proj(ffn_w, 32, "plain", dstd=out_fm, resid=st_x3,
               bias=lambda oc: fbias[:, oc:oc + 1], lora=(lora_b, hA))

    global LAST_CNT
    LAST_CNT = dict(cnt)
    tots_f = tot_stores()
    tot_cc = cnt["s_cc"]
    with nc.Block() as blk:
        @blk.gpsimd
        def _(g):
            for sem, vv in tots_f:
                g.wait_ge(sem, vv)
            g.wait_ge(S["s_cc"], tot_cc)

    return nc


_NC_CACHE = {}


def kernel(**inputs):
    percore = host_prep(inputs)
    key = "full"
    if key not in _NC_CACHE:
        _NC_CACHE[key] = build_program()
    nc = _NC_CACHE[key]
    res = run_bass_kernel_spmd(nc, percore, core_ids=list(range(N_CORES)))
    out = np.empty((2, 4096, 4096), np.float32)
    for c in range(N_CORES):
        b, j = c // 4, c % 4
        tok = _tok_idx(j)
        out[b, tok, :] = res.results[c]["out_fm"].reshape(4096, NT).T
    return out


# version 08 (final)
# speedup vs baseline: 1.0618x; 1.0618x over previous
import numpy as np
import ml_dtypes
import concourse.bass as bass
import concourse.mybir as mybir
from concourse.bass_utils import run_bass_kernel_spmd

F32 = mybir.dt.float32
BF16 = mybir.dt.bfloat16
AF = mybir.ActivationFunctionType
ALU = mybir.AluOpType

NT = 1024
NEG = -1.0e30
N_CORES = 8
LAST_CNT = {}
RG = [[0, 1, 2, 3], [4, 5, 6, 7]]


def _runs_for(j):
    return [(512 * j, 512 * (j + 1)), (4096 - 512 * (j + 1), 4096 - 512 * j)]


def _tok_idx(j):
    (a0, a1), (b0, b1) = _runs_for(j)
    return np.concatenate([np.arange(a0, a1), np.arange(b0, b1)])


def _run_owner(g):
    return (g, 0) if g < 4 else (7 - g, 1)


def _slabs(w):
    o, i = w.shape
    wt = np.ascontiguousarray(w.T)
    return np.ascontiguousarray(
        wt.reshape(i // 128, 128, o // 128, 128).transpose(2, 1, 0, 3)
        .reshape(o // 128, 128, (i // 128) * 128))


def _rope_tables(pos):
    inv = 1.0 / (10000.0 ** (np.arange(0, 128, 2, dtype=np.float64) / 128.0))
    ang = pos[None, :].astype(np.float64) * inv[:, None]
    ang2 = np.concatenate([ang, ang], axis=0)
    return np.cos(ang2), np.sin(ang2)


def _hi_lo(a64):
    hi = a64.astype(np.float32)
    return hi, (a64 - hi.astype(np.float64)).astype(np.float32)


def host_prep(inputs):
    g = {}
    wa = np.asarray(inputs["L0_attn_a_w_attn"])
    g["wqk"] = _slabs(wa[:8192])
    g["wv_a"] = _slabs(wa[8192:12288])
    g["wproj"] = _slabs(np.asarray(inputs["L0_attn_a_w_proj"]))
    g["adapter_fm"] = np.ascontiguousarray(
        np.asarray(inputs["L0_attn_a_adapter_wte"]).T.reshape(32, 128, 10)
        .transpose(1, 0, 2).reshape(128, 320))
    g["gating"] = np.asarray(inputs["L0_attn_a_gating"]).reshape(1, 1)
    g["wq_b"] = _slabs(np.asarray(inputs["L0_attn_b_Wq"]))
    g["wk_b"] = _slabs(np.asarray(inputs["L0_attn_b_Wk"]))
    g["wv_b"] = _slabs(np.asarray(inputs["L0_attn_b_Wv"]))
    g["wo_b"] = _slabs(np.asarray(inputs["L0_attn_b_Wo"]))
    g["wg_b"] = _slabs(np.asarray(inputs["L0_attn_b_Wg"]))
    g["win"] = np.ascontiguousarray(
        np.asarray(inputs["L0_attn_b_Win"]).T.reshape(32, 128, 32)
        .transpose(1, 0, 2).reshape(128, 1024))
    g["dt_bias"] = np.asarray(inputs["L0_attn_b_dt_bias"]).reshape(32, 1)
    g["alog"] = np.asarray(inputs["L0_attn_b_A_log_bias"]).reshape(32, 1)
    g["gnorm"] = np.asarray(inputs["L0_attn_b_gnorm_w"]).reshape(128, 1)
    g["lora_a"] = _slabs(np.asarray(inputs["L0_ffn_lora_A"]).reshape(128, 4096))
    Ball = np.asarray(inputs["L0_ffn_lora_B"]).transpose(1, 0, 2).reshape(4096, 128)
    g["lora_b"] = _slabs(Ball)
    g["ffn_w"] = _slabs(np.asarray(inputs["L0_ffn_W"]))
    g["ffn_bias"] = np.ascontiguousarray(
        np.asarray(inputs["L0_ffn_bias"]).reshape(32, 128).T)
    g["lora_w"] = np.asarray(inputs["L0_ffn_lora_weights"]).reshape(8, 1)
    E = np.zeros((8, 128), np.float32)
    for n in range(8):
        E[n, n * 16:(n + 1) * 16] = 2.0
    g["E8"] = E
    g["ident"] = np.eye(128, dtype=np.float32)
    g["ident32s"] = np.tile(np.eye(32, dtype=np.float32), (4, 1))
    g["ones_col"] = np.ones((128, 1), np.float32)
    g["ones_row"] = np.ones((1, 128), np.float32)
    g["ones128"] = np.ones((128, 128), np.float32)
    jj = np.arange(128)[:, None]
    ii = np.arange(128)[None, :]
    g["dmaskT"] = np.where(ii >= jj, 0.0, NEG).astype(np.float32)

    x = np.asarray(inputs["L0_attn_a_x"])
    percore = []
    kk = np.arange(128)[:, None]
    qq = np.arange(512)[None, :]
    for c in range(N_CORES):
        b, j = c // 4, c % 4
        tok = _tok_idx(j)
        d = dict(g)
        d["x_fm"] = np.ascontiguousarray(x[b][tok].T).reshape(32, 128, NT)
        cos64, sin64 = _rope_tables(tok)
        s = 1.0 / np.sqrt(np.float64(128.0))
        for nm, t64 in (("cq", cos64 * s), ("sq", sin64 * s),
                        ("ck", cos64), ("sk", sin64)):
            hi, lo = _hi_lo(t64)
            d[nm + "h"], d[nm + "l"] = hi, lo
        mt = np.full((2, 32, 128, 512), NEG, np.float32)
        for qb in range(2):
            Q0 = 512 * j if qb == 0 else 4096 - 512 * (j + 1)
            for sl in range(32 if qb else 16):
                mt[qb, sl] = np.where(128 * sl + kk <= Q0 + qq, 0.0, NEG)
        d["mask_tbl"] = mt.astype(ml_dtypes.bfloat16)
        selA = np.zeros((1, 8), np.float32)
        selB = np.zeros((1, 8), np.float32)
        if j >= 1:
            selA[0, j - 1] = 1.0
        selB[0, 6 - j] = 1.0
        d["selAB"] = np.concatenate([selA, selB], axis=1)
        percore.append(d)
    return percore


def build_program(stop_after=99, taps=()):
    nc = bass.Bass("TRN2", target_bir_lowering=False,
                   detect_race_conditions=False)

    def din(name, shape, dt=F32):
        return nc.dram_tensor(name, shape, dt, kind="ExternalInput").ap()

    def dstg(name, shape):
        if name in taps:
            return nc.dram_tensor(name, shape, F32, kind="ExternalOutput").ap()
        return nc.dram_tensor(name, shape, F32).ap()

    x_fm = din("x_fm", [32, 128, NT])
    wqk = din("wqk", [64, 128, 4096])
    wv_a = din("wv_a", [32, 128, 4096])
    wproj = din("wproj", [32, 128, 4096])
    adapter_fm = din("adapter_fm", [128, 320])
    gating_in = din("gating", [1, 1])
    wq_b = din("wq_b", [32, 128, 4096])
    wk_b = din("wk_b", [8, 128, 4096])
    wv_b = din("wv_b", [8, 128, 4096])
    wo_b = din("wo_b", [32, 128, 4096])
    wg_b = din("wg_b", [32, 128, 4096])
    win = din("win", [128, 1024])
    dt_bias = din("dt_bias", [32, 1])
    alog = din("alog", [32, 1])
    gnorm_in = din("gnorm", [128, 1])
    lora_a = din("lora_a", [1, 128, 4096])
    lora_b = din("lora_b", [32, 128, 128])
    ffn_w = din("ffn_w", [32, 128, 4096])
    ffn_bias = din("ffn_bias", [128, 32])
    lora_w = din("lora_w", [8, 1])
    E8_in = din("E8", [8, 128])
    ident_in = din("ident", [128, 128])
    ident32s_in = din("ident32s", [128, 32])
    ones_col_in = din("ones_col", [128, 1])
    ones_row_in = din("ones_row", [1, 128])
    ones128_in = din("ones128", [128, 128])
    dmaskT_in = din("dmaskT", [128, 128])
    mask_tbl = din("mask_tbl", [2, 32, 128, 512], BF16)
    selAB_in = din("selAB", [1, 16])
    tab_in = {nm: din(nm, [128, NT]) for nm in
              ("cqh", "cql", "sqh", "sql", "ckh", "ckl", "skh", "skl")}

    st_x1 = dstg("st_x1", [32, 128, NT])
    st_x2 = dstg("st_x2", [32, 128, NT])
    st_x3 = dstg("st_x3", [32, 128, NT])
    st_qa = dstg("st_qa", [32, 128, NT])
    st_ya = dstg("st_ya", [32, 128, NT])
    st_qb = dstg("st_qb", [32, 128, NT])
    st_gate = dstg("st_gate", [32, 128, NT])
    st_yb = dstg("st_yb", [32, 128, NT])
    st_kfm = dstg("st_kfm", [8, 128, NT])
    st_kkm = dstg("st_kkm", [8, 8, 128, 128])
    st_vkm = dstg("st_vkm", [8, 8, 128, 128])
    st_akm = dstg("st_akm", [32, 128, 16])
    st_avk = dstg("st_avk", [32, 10, 128])
    st_ln = dstg("st_ln", [32, 128, NT])
    st_hg = dstg("st_hg", [128, NT])
    ctr_k = dstg("ctr_k", [32, 128, NT])
    ctr_v = dstg("ctr_v", [32, 8, 128, 128])
    k_all = dstg("k_all", [4, 32, 128, NT])
    v_all = dstg("v_all", [4, 32, 8, 128, 128])
    ctr_s = dstg("ctr_s", [2, 32, 128, 128])
    s_all = dstg("s_all", [4, 2, 32, 128, 128])
    ctr_d = dstg("ctr_d", [2, 32])
    d_all = dstg("d_all", [4, 2, 32])
    out_fm = nc.dram_tensor("out_fm", [32, 128, NT], F32,
                            kind="ExternalOutput").ap()

    def sbt(name, shape, dt=F32):
        return nc.alloc_sbuf_tensor(name, shape, dt).ap()

    arena = sbt("arena", [128, 32 * NT])
    abf = arena.bitcast(BF16)
    stg = [sbt("stg0", [128, 2048]), sbt("stg1", [128, 2048])]
    whb = [sbt("whb0", [128, 2048], BF16), sbt("whb1", [128, 2048], BF16)]
    wlb = [sbt("wlb0", [128, 2048], BF16), sbt("wlb1", [128, 2048], BF16)]
    lstg = [sbt("lstg0", [128, 128]), sbt("lstg1", [128, 128])]
    tabs = {nm: sbt("t_" + nm, [128, NT]) for nm in ("cq", "sq", "ck", "sk")}
    bc_m = sbt("bc_m", [128, NT])
    bc_r = sbt("bc_r", [128, NT])
    dmaskT = sbt("dmaskT_s", [128, 128])
    ident = sbt("ident_s", [128, 128])
    ident32s = sbt("ident32s_s", [128, 32])
    ones128 = sbt("ones128_s", [128, 128])
    ones_col = sbt("ones_c", [128, 1])
    ones_row = sbt("ones_r", [1, 128])
    misc = sbt("misc", [128, 2048])
    obuf = sbt("obuf", [128, 2048])
    small = sbt("small", [128, 64])
    adp = sbt("adp", [128, 320])
    holdG = arena[:, 0:NT]
    holdT = arena[:, NT:NT + 512]
    hA = obuf[:, 1024:2048]
    fbias = sbt("fbias", [128, 32])
    selbc = sbt("selbc", [128, 16])

    ps = [nc.alloc_psum_tensor("ps%d" % i, [128, 512], F32).ap()
          for i in range(8)]

    col_gn = small[:, 0:1]
    col_dtb = small[0:32, 1:2]
    col_alog = small[0:32, 2:3]
    val_gat = small[0:1, 3:4]
    col_coef = small[:, 4:5]
    col_lw = small[0:8, 5:6]
    col_rs2 = small[:, 6:7]
    col_gat = small[:, 7:8]
    zcol = small[32:64, 8:9]
    col_eps = small[:, 14:15]
    col_m30 = small[:, 15:16]
    col_m10 = small[:, 16:17]
    row_sel = small[0:1, 24:40]
    onescb = small.bitcast(BF16)[:, 34:35]

    S = {nm: nc.alloc_semaphore(name=nm) for nm in
         ("s_w0", "s_w1", "s_x", "s_x0", "s_x1", "s_mm", "s_a", "s_v",
          "s_st0", "s_st1", "s_st2", "s_st3", "s_sta0", "s_sta1",
          "s_mk0", "s_mk1", "s_d0", "s_d1", "s_g", "s_cc",
          "s_h", "s_l", "s_pw")}
    cnt = {nm: 0 for nm in S}

    def SW(p):
        return S["s_w%d" % (p % 2)]

    def SX(p):
        return S["s_x%d" % (p % 2)]

    def SST(p):
        return S["s_st%d" % p]

    def tot_stores():
        return [(S["s_st%d" % i], cnt["s_st%d" % i]) for i in range(4)] + \
               [(S["s_sta%d" % i], cnt["s_sta%d" % i]) for i in range(2)]
    PH = [0]

    def on():
        PH[0] += 1
        return PH[0] <= stop_after

    if on():
        bx, bv = cnt["s_x"], cnt["s_v"]
        with nc.Block() as blk:
            @blk.sync
            def _(e):
                lo_slots = [misc[:, 0:NT], misc[:, NT:2 * NT],
                            obuf[:, 0:NT], obuf[:, NT:2 * NT]]
                for nm in ("cq", "sq", "ck", "sk"):
                    e.dma_start(tabs[nm], tab_in[nm + "h"]).then_inc(S["s_x"], 16)
                for i, nm in enumerate(("cql", "sql", "ckl", "skl")):
                    e.dma_start(lo_slots[i], tab_in[nm]).then_inc(S["s_x"], 16)
                for dap, src in [
                        (dmaskT, dmaskT_in), (ident, ident_in),
                        (ident32s, ident32s_in),
                        (ones128, ones128_in), (ones_col, ones_col_in),
                        (ones_row, ones_row_in), (col_gn, gnorm_in),
                        (col_dtb, dt_bias), (col_alog, alog),
                        (val_gat, gating_in), (col_lw, lora_w),
                        (row_sel, selAB_in),
                        (fbias, ffn_bias), (adp, adapter_fm)]:
                    e.dma_start(dap, src).then_inc(S["s_x"], 16)

            @blk.vector
            def _(v):
                lo_slots = [misc[:, 0:NT], misc[:, NT:2 * NT],
                            obuf[:, 0:NT], obuf[:, NT:2 * NT]]
                names = ("cq", "sq", "ck", "sk")
                v.wait_ge(S["s_x"], bx + 16 * 22)
                for i in range(4):
                    v.tensor_tensor(tabs[names[i]], tabs[names[i]],
                                    lo_slots[i], op=ALU.add) \
                        .then_inc(S["s_v"], 1)
                v.memset(zcol, 0.0)
                v.memset(onescb, 1.0)
                v.memset(col_eps, 1e-5)
                v.memset(col_m30, -30.0)
                v.memset(col_m10, -10.0)
                v.memset(col_rs2, 2.0).then_inc(S["s_v"], 1)
        cnt["s_x"] += 16 * 22
        cnt["s_v"] += 5

        bmm, bv = cnt["s_mm"], cnt["s_v"]
        bx2 = cnt["s_x"]
        with nc.Block() as blk:
            @blk.sync
            def _(e):
                e.dma_start(misc[0:8, 0:128], E8_in).then_inc(S["s_x"], 16)

            @blk.tensor
            def _(t):
                t.wait_ge(S["s_x"], bx2 + 16)
                t.matmul(ps[0][:, 0:1], misc[0:8, 0:128], col_lw,
                         start=True, stop=True)
                t.matmul(ps[1][:, 0:1], ones_row, val_gat,
                         start=True, stop=True)
                t.matmul(ps[2][:, 0:16], ones_row, row_sel,
                         start=True, stop=True).then_inc(S["s_mm"], 1)

            @blk.scalar
            def _(s):
                s.activation(col_alog, col_alog, AF.Exp)
                s.mul(col_alog, col_alog, -1.0)
                s.activation(col_rs2, col_rs2, AF.Sqrt)
                s.mul(col_rs2, col_rs2, 0.125).then_inc(S["s_a"], 1)

            @blk.vector
            def _(v):
                v.wait_ge(S["s_mm"], bmm + 1)
                v.tensor_copy(col_coef, ps[0][:, 0:1])
                v.tensor_copy(col_gat, ps[1][:, 0:1])
                v.tensor_copy(selbc, ps[2][:, 0:16])
        cnt["s_mm"] += 1
        cnt["s_a"] += 1
        cnt["s_x"] += 16

    def p_ln(src, tap=False):
        xb = [misc[:, 0:NT], obuf[:, 0:NT]]
        sq = [arena[:, 0:NT], arena[:, NT:2 * NT]]
        bx0, bx1 = cnt["s_x0"], cnt["s_x1"]
        ba, bmm = cnt["s_a"], cnt["s_mm"]

        def xw(kc):
            return (SX(kc), (bx0 if kc % 2 == 0 else bx1) + 16 * (kc // 2 + 1))
        with nc.Block() as blk:
            @blk.sync
            def _(e):
                for kc in range(32):
                    if kc >= 2:
                        e.wait_ge(S["s_mm"], bmm + (kc - 1) * 4)
                    e.dma_start(xb[kc % 2], src[kc]).then_inc(SX(kc), 16)

            @blk.tensor
            def _(t):
                for kc in range(32):
                    t.wait_ge(*xw(kc))
                    st, sp = kc == 0, kc == 31
                    t.matmul(ps[4][0:1, :], ones_col, xb[kc % 2][:, 0:512],
                             start=st, stop=sp)
                    t.matmul(ps[5][0:1, :], ones_col, xb[kc % 2][:, 512:1024],
                             start=st, stop=sp)
                    t.wait_ge(S["s_a"], ba + kc + 1)
                    t.matmul(ps[6][0:1, :], ones_col, sq[kc % 2][:, 0:512],
                             start=st, stop=sp)
                    t.matmul(ps[7][0:1, :], ones_col, sq[kc % 2][:, 512:1024],
                             start=st, stop=sp).then_inc(S["s_mm"], 4)

            @blk.scalar
            def _(s):
                for kc in range(32):
                    s.wait_ge(*xw(kc))
                    if kc >= 2:
                        s.wait_ge(S["s_mm"], bmm + (kc - 1) * 4)
                    s.activation(sq[kc % 2], xb[kc % 2], AF.Square) \
                        .then_inc(S["s_a"], 1)
        cnt["s_x0"] += 16 * 16
        cnt["s_x1"] += 16 * 16
        cnt["s_mm"] += 128
        cnt["s_a"] += 32

        r_m = misc[0:1, 0:NT]
        r_r = misc[0:1, NT:2 * NT]
        r_t = obuf[0:1, 1024:2048]
        bv, ba, bmm = cnt["s_v"], cnt["s_a"], cnt["s_mm"]
        with nc.Block() as blk:
            @blk.vector
            def _(v):
                v.tensor_scalar(r_m[0:1, 0:512], ps[4][0:1, :], 1.0 / 4096,
                                None, op0=ALU.mult)
                v.tensor_scalar(r_m[0:1, 512:1024], ps[5][0:1, :], 1.0 / 4096,
                                None, op0=ALU.mult)
                v.tensor_scalar(r_r[0:1, 0:512], ps[6][0:1, :], 1.0 / 4096,
                                None, op0=ALU.mult)
                v.tensor_scalar(r_r[0:1, 512:1024], ps[7][0:1, :], 1.0 / 4096,
                                None, op0=ALU.mult).then_inc(S["s_v"], 1)
                v.wait_ge(S["s_a"], ba + 1)
                v.tensor_tensor(r_r, r_r, r_t, op=ALU.subtract) \
                    .then_inc(S["s_v"], 1)
                v.wait_ge(S["s_a"], ba + 2)
                v.reciprocal(r_r, r_r).then_inc(S["s_v"], 1)

            @blk.scalar
            def _(s):
                s.wait_ge(S["s_v"], bv + 1)
                s.activation(r_t, r_m, AF.Square).then_inc(S["s_a"], 1)
                s.wait_ge(S["s_v"], bv + 2)
                s.activation(r_r, r_r, AF.Sqrt, bias=col_eps[0:1, :]).then_inc(S["s_a"], 1)

            @blk.tensor
            def _(t):
                t.wait_ge(S["s_v"], bv + 3)
                for half in range(2):
                    hs = slice(half * 512, (half + 1) * 512)
                    t.matmul(ps[half * 2], ones_row, r_m[0:1, hs],
                             start=True, stop=True)
                    t.matmul(ps[half * 2 + 1], ones_row, r_r[0:1, hs],
                             start=True, stop=True).then_inc(S["s_mm"], 1)

            @blk.vector
            def _(v):
                for half in range(2):
                    hs = slice(half * 512, (half + 1) * 512)
                    v.wait_ge(S["s_mm"], bmm + half + 1)
                    v.tensor_copy(bc_m[:, hs], ps[half * 2])
                    v.tensor_copy(bc_r[:, hs], ps[half * 2 + 1]) \
                        .then_inc(S["s_v"], 1)
        cnt["s_v"] += 5
        cnt["s_a"] += 2
        cnt["s_mm"] += 2

        bx0, bx1, bv = cnt["s_x0"], cnt["s_x1"], cnt["s_v"]

        def xw2(kc):
            return (SX(kc), (bx0 if kc % 2 == 0 else bx1) + 16 * (kc // 2 + 1))
        with nc.Block() as blk:
            @blk.sync
            def _(e):
                for kc in range(32):
                    if kc >= 2:
                        e.wait_ge(S["s_v"], bv + kc - 1)
                    e.dma_start(xb[kc % 2], src[kc]).then_inc(SX(kc), 16)

            @blk.vector
            def _(v):
                for kc in range(32):
                    v.wait_ge(*xw2(kc))
                    hi = abf[:, kc * 2048:kc * 2048 + 1024]
                    lo = abf[:, kc * 2048 + 1024:(kc + 1) * 2048]
                    v.tensor_tensor(xb[kc % 2], xb[kc % 2], bc_m,
                                    op=ALU.subtract)
                    v.tensor_tensor(xb[kc % 2], xb[kc % 2], bc_r, op=ALU.mult)
                    v.tensor_copy(hi, xb[kc % 2])
                    v.tensor_tensor(lo, xb[kc % 2], hi, op=ALU.subtract) \
                        .then_inc(S["s_v"], 1)
        cnt["s_x0"] += 16 * 16
        cnt["s_x1"] += 16 * 16
        cnt["s_v"] += 32

    def rhs_pair(kc, nt):
        b = kc * 2048 + nt * 512
        return abf[:, b:b + 512], abf[:, b + 1024:b + 1536]

    def p_proj(slabs, n_oc, kind, dstd=None, tabpair=None, resid=None,
               bias=None, lora=None, adapter=None, oc0=0):
        VINC = {"rope": 1, "plain": 1, "vtp": 2, "hA": 1}[kind]
        P = 2 + (1 if adapter else 0) + (2 if kind == "vtp" else 0)
        bw = [cnt["s_w0"], cnt["s_w1"]]
        bxp = [cnt["s_x0"], cnt["s_x1"]]
        bstp = [cnt["s_st0"], cnt["s_st1"]]
        bsta = [cnt["s_sta0"], cnt["s_sta1"]]
        bmk = [cnt["s_mk0"], cnt["s_mk1"]]
        bmm, bv, bg = cnt["s_mm"], cnt["s_v"], cnt["s_g"]
        bh, bl, bpw = cnt["s_h"], cnt["s_l"], cnt["s_pw"]
        rbuf = [misc[:, 0:NT], misc[:, NT:2 * NT]]
        ak_ob = [obuf[:, 1024:1040], obuf[:, 1040:1056]]
        av_ob = [misc[0:10, 512:640], misc[0:10, 640:768]]
        nH = 2 * n_oc

        def upos(u):
            return bmm + (u // 2) * P + (u % 2) + 1

        def adpos(i):
            return bmm + i * P + 3

        def tppos(u):
            return bmm + (u // 2) * P + 2 + (1 if adapter else 0) + (u % 2) + 1

        def dma_w(e, H):
            if bl + H >= 2:
                e.wait_ge(S["s_l"], bl + H - 1)
            if bpw + H >= 2:
                e.wait_ge(S["s_pw"], bpw + H - 1)
            i, h = H // 2, H % 2
            e.dma_start(stg[H % 2],
                        slabs[oc0 + i][:, h * 2048:(h + 1) * 2048]) \
                .then_inc(SW(H), 16)

        def dma_resid(e, i):
            if i >= 2:
                e.wait_ge(S["s_v"], bv + VINC * (2 * i - 2))
            else:
                e.wait_ge(S["s_v"], bv)
            e.dma_start(rbuf[i % 2], resid[oc0 + i]).then_inc(SX(i), 16)

        def split_hi(s, H):
            if bpw + H >= 2:
                s.wait_ge(S["s_pw"], bpw + H - 1)
            s.wait_ge(SW(H), bw[H % 2] + 16 * (H // 2 + 1))
            s.activation(whb[H % 2], stg[H % 2], AF.Copy) \
                .then_inc(S["s_h"], 1)

        def split_lo(v, H):
            v.wait_ge(S["s_h"], bh + H + 1)
            if bpw + H >= 2:
                v.wait_ge(S["s_pw"], bpw + H - 1)
            v.tensor_tensor(wlb[H % 2], stg[H % 2], whb[H % 2],
                            op=ALU.subtract).then_inc(S["s_l"], 1)

        with nc.Block() as blk:
            @blk.sync
            def _(e):
                for H in range(min(4, nH)):
                    dma_w(e, H)
                for i in range(n_oc):
                    if lora is not None:
                        if i >= 2:
                            e.wait_ge(S["s_mm"], bmm + (i - 1) * P)
                        e.dma_start(lstg[i % 2], lora[0][oc0 + i]) \
                            .then_inc(S["s_mk%d" % (i % 2)], 16)
                    if resid is not None and i >= 1:
                        dma_resid(e, i - 1)
                    for H in (2 * i + 4, 2 * i + 5):
                        if H < nH:
                            dma_w(e, H)
                if resid is not None:
                    dma_resid(e, n_oc - 1)

            @blk.tensor
            def _(t):
                for i in range(n_oc):
                    for h in range(2):
                        H = 2 * i + h
                        t.wait_ge(S["s_l"], bl + H + 1)
                        last = None
                        busy = False
                        for nt in range(2):
                            u = 2 * i + nt
                            bank = ps[u % 4]
                            if h == 0 and u >= 4:
                                t.wait_ge(S["s_v"], bv + VINC * (u - 4) + 1)
                            for kc in range(16 * h, 16 * h + 16):
                                c = (kc - 16 * h) * 128
                                wh = whb[H % 2][:, c:c + 128]
                                wl = wlb[H % 2][:, c:c + 128]
                                rh, rl = rhs_pair(kc, nt)
                                t.matmul(bank, wh, rh, start=(kc == 0),
                                         stop=False)
                                t.matmul(bank, wh, rl, start=False, stop=False)
                                sp = (h == 1 and kc == 31 and lora is None)
                                last = t.matmul(bank, wl, rh, start=False,
                                                stop=sp)
                                if sp:
                                    last.then_inc(S["s_mm"], 1)
                                    busy = True
                                else:
                                    busy = False
                        if adapter is not None:
                            if i >= 1 and h == 0:
                                t.wait_ge(S["s_g"], bg + i)
                            for kc in range(16 * h, 16 * h + 16):
                                c = (kc - 16 * h) * 128
                                tile = stg[H % 2][:, c:c + 128]
                                cst, csp = (kc == 0), (kc == 31)
                                if adapter == "ak":
                                    am = t.matmul(ps[6][:, 0:10], tile,
                                                  adp[:, kc * 10:(kc + 1) * 10],
                                                  start=cst, stop=csp)
                                else:
                                    am = t.matmul(ps[7][0:10, 0:128],
                                                  adp[:, kc * 10:(kc + 1) * 10],
                                                  tile, start=cst, stop=csp)
                            busy = False
                            if h == 1:
                                am.then_inc(S["s_mm"], 1)
                                busy = True
                            last = am
                        if busy:
                            t.matmul(ps[5][0:1, 0:1], ones_col[0:1, :],
                                     ones_col[0:1, :], start=True,
                                     stop=True).then_inc(S["s_pw"], 1)
                        else:
                            last.then_inc(S["s_pw"], 1)
                    if lora is not None:
                        t.wait_ge(S["s_mk%d" % (i % 2)],
                                  bmk[i % 2] + 16 * (i // 2 + 1))
                        for nt in range(2):
                            u = 2 * i + nt
                            t.matmul(ps[u % 4], lstg[i % 2],
                                     lora[1][:, nt * 512:(nt + 1) * 512],
                                     start=False, stop=True) \
                                .then_inc(S["s_mm"], 1)
                    if kind == "vtp":
                        for nt in range(2):
                            u = 2 * i + nt
                            t.wait_ge(S["s_v"], bv + VINC * u + 1)
                            if u >= 1:
                                t.wait_ge(S["s_v"], bv + VINC * (u - 1) + 2)
                            for q4 in range(4):
                                tp = t.transpose(
                                    ps[4][:, q4 * 128:(q4 + 1) * 128],
                                    obuf[:, (u % 2) * 512 + q4 * 128:
                                         (u % 2) * 512 + (q4 + 1) * 128],
                                    ident)
                            tp.then_inc(S["s_mm"], 1)

            @blk.scalar
            def _(s):
                for H in range(min(4, nH)):
                    split_hi(s, H)
                for i in range(n_oc):
                    oc = oc0 + i
                    for nt in range(2):
                        u = 2 * i + nt
                        if kind in ("rope", "plain") and dstd is not None:
                            s.wait_ge(S["s_v"], bv + VINC * u + 1)
                            s.dma_start(dstd[oc % 32][:, nt * 512:(nt + 1) * 512],
                                        obuf[:, (u % 2) * 512:(u % 2 + 1) * 512]) \
                                .then_inc(SST(u % 2), 16)
                        elif kind == "vtp":
                            s.wait_ge(S["s_v"], bv + VINC * u + 2)
                            s.dma_start(
                                dstd[oc % 32, nt * 4:(nt + 1) * 4]
                                .transpose([1, 0, 2]),
                                obuf[:, (2 + u % 2) * 512:(3 + u % 2) * 512]
                                .rearrange("p (c d) -> p c d", c=4)) \
                                .then_inc(SST(u % 2), 16)
                    if adapter == "ak":
                        s.wait_ge(S["s_g"], bg + i + 1)
                        s.dma_start(st_akm[oc0 + i - 32][:, 0:16],
                                    ak_ob[i % 2]) \
                            .then_inc(S["s_sta%d" % (i % 2)], 16)
                    elif adapter == "av":
                        s.wait_ge(S["s_g"], bg + i + 1)
                        s.dma_start(st_avk[i], av_ob[i % 2]) \
                            .then_inc(S["s_sta%d" % (i % 2)], 16)
                    for H in (2 * i + 4, 2 * i + 5):
                        if H < nH:
                            split_hi(s, H)

            @blk.vector
            def _(v):
                for H in range(min(4, nH)):
                    split_lo(v, H)
                for i in range(n_oc):
                    oc = oc0 + i
                    for nt in range(2):
                        u = 2 * i + nt
                        bank = ps[u % 4]
                        v.wait_ge(S["s_mm"], upos(u))
                        if u >= 2 and dstd is not None:
                            v.wait_ge(SST(u % 2), bstp[u % 2] + 16 * (u // 2))
                        if kind == "vtp" and u >= 2:
                            v.wait_ge(S["s_mm"], tppos(u - 2))
                        ob = obuf[:, (u % 2) * 512:(u % 2 + 1) * 512]
                        if kind == "rope":
                            cosT, sinT = tabpair
                            tsl = slice(nt * 512, (nt + 1) * 512)
                            tmp_lo = misc[0:64, 1536:2048]
                            tmp_hi = misc[64:128, 1536:2048]
                            v.tensor_tensor(ob[0:64, :], bank[0:64, :],
                                            cosT[0:64, tsl], op=ALU.mult)
                            v.tensor_tensor(tmp_lo, bank[64:128, :],
                                            sinT[0:64, tsl], op=ALU.mult)
                            v.tensor_tensor(ob[0:64, :], ob[0:64, :], tmp_lo,
                                            op=ALU.subtract)
                            v.tensor_tensor(ob[64:128, :], bank[64:128, :],
                                            cosT[64:128, tsl], op=ALU.mult)
                            v.tensor_tensor(tmp_hi, bank[0:64, :],
                                            sinT[64:128, tsl], op=ALU.mult)
                            v.tensor_tensor(ob[64:128, :], ob[64:128, :],
                                            tmp_hi, op=ALU.add) \
                                .then_inc(S["s_v"], 1)
                        elif kind == "plain":
                            if resid is not None:
                                v.wait_ge(SX(i), bxp[i % 2] + 16 * (i // 2 + 1))
                                if bias is not None:
                                    v.scalar_tensor_tensor(
                                        ob, bank, bias(oc),
                                        rbuf[i % 2][:, nt * 512:(nt + 1) * 512],
                                        op0=ALU.add, op1=ALU.add) \
                                        .then_inc(S["s_v"], 1)
                                else:
                                    v.tensor_tensor(
                                        ob, bank,
                                        rbuf[i % 2][:, nt * 512:(nt + 1) * 512],
                                        op=ALU.add).then_inc(S["s_v"], 1)
                            else:
                                v.tensor_copy(ob, bank).then_inc(S["s_v"], 1)
                        elif kind == "vtp":
                            v.tensor_copy(ob, bank).then_inc(S["s_v"], 1)
                            v.wait_ge(S["s_mm"], tppos(u))
                            v.tensor_copy(
                                obuf[:, (2 + u % 2) * 512:(3 + u % 2) * 512],
                                ps[4]).then_inc(S["s_v"], 1)
                        elif kind == "hA":
                            v.tensor_scalar(hA[:, nt * 512:(nt + 1) * 512],
                                            bank, col_coef, None,
                                            op0=ALU.mult).then_inc(S["s_v"], 1)
                    if adapter == "ak":
                        v.wait_ge(S["s_mm"], adpos(i))
                        if i >= 2:
                            v.wait_ge(S["s_sta%d" % (i % 2)],
                                      bsta[i % 2] + 16 * (i // 2))
                        v.memset(ak_ob[i % 2], 0.0)
                        v.tensor_scalar(ak_ob[i % 2][:, 0:10], ps[6][:, 0:10],
                                        col_rs2, None, op0=ALU.mult) \
                            .then_inc(S["s_g"], 1)
                    elif adapter == "av":
                        v.wait_ge(S["s_mm"], adpos(i))
                        if i >= 2:
                            v.wait_ge(S["s_sta%d" % (i % 2)],
                                      bsta[i % 2] + 16 * (i // 2))
                        v.tensor_copy(av_ob[i % 2], ps[7][0:10, 0:128]) \
                            .then_inc(S["s_g"], 1)
                    for H in (2 * i + 4, 2 * i + 5):
                        if H < nH:
                            split_lo(v, H)

        for i in range(n_oc):
            if resid is not None:
                cnt["s_x%d" % (i % 2)] += 16
            if lora is not None:
                cnt["s_mk%d" % (i % 2)] += 16
        cnt["s_w0"] += 16 * n_oc
        cnt["s_w1"] += 16 * n_oc
        cnt["s_h"] += nH
        cnt["s_l"] += nH
        cnt["s_pw"] += nH
        cnt["s_mm"] += n_oc * P
        cnt["s_v"] += VINC * 2 * n_oc
        if kind in ("rope", "plain", "vtp") and dstd is not None:
            cnt["s_st0"] += 16 * n_oc
            cnt["s_st1"] += 16 * n_oc
        if adapter is not None:
            cnt["s_g"] += n_oc
            for i in range(n_oc):
                cnt["s_sta%d" % (i % 2)] += 16

    def barrier_stores():
        tots = tot_stores()
        with nc.Block() as blk:
            @blk.gpsimd
            def _(g):
                for sem, v in tots:
                    g.wait_ge(sem, v)

    def p_load_arena(src, n=32):
        bx0, bx1, bv = cnt["s_x0"], cnt["s_x1"], cnt["s_v"]
        xb = [misc[:, 0:NT], obuf[:, 0:NT]]

        def xw(kc):
            return (SX(kc), (bx0 if kc % 2 == 0 else bx1) + 16 * (kc // 2 + 1))
        with nc.Block() as blk:
            @blk.sync
            def _(e):
                for kc in range(n):
                    if kc >= 2:
                        e.wait_ge(S["s_v"], bv + kc - 1)
                    e.dma_start(xb[kc % 2], src[kc]).then_inc(SX(kc), 16)

            @blk.vector
            def _(v):
                for kc in range(n):
                    v.wait_ge(*xw(kc))
                    hi = abf[:, kc * 2048:kc * 2048 + 1024]
                    lo = abf[:, kc * 2048 + 1024:(kc + 1) * 2048]
                    v.tensor_copy(hi, xb[kc % 2])
                    v.tensor_tensor(lo, xb[kc % 2], hi, op=ALU.subtract) \
                        .then_inc(S["s_v"], 1)
        cnt["s_x0"] += 16 * (n - n // 2)
        cnt["s_x1"] += 16 * (n // 2)
        cnt["s_v"] += n

    def p_dt():
        bw0, bw1 = cnt["s_w0"], cnt["s_w1"]
        bmm, bv = cnt["s_mm"], cnt["s_v"]
        bh, bl, bpw = cnt["s_h"], cnt["s_l"], cnt["s_pw"]
        with nc.Block() as blk:
            @blk.sync
            def _(e):
                e.wait_ge(S["s_l"], bl)
                e.wait_ge(S["s_pw"], bpw)
                e.dma_start(stg[0][:, 0:512], win[:, 0:512]) \
                    .then_inc(S["s_w0"], 16)
                e.dma_start(stg[1][:, 0:512], win[:, 512:1024]) \
                    .then_inc(S["s_w1"], 16)

            @blk.scalar
            def _(s):
                for p in range(2):
                    s.wait_ge(S["s_w%d" % p], (bw0 if p == 0 else bw1) + 16)
                    s.activation(whb[p][:, 0:512], stg[p][:, 0:512], AF.Copy) \
                        .then_inc(S["s_h"], 1)

            @blk.vector
            def _(v):
                for p in range(2):
                    v.wait_ge(S["s_h"], bh + p + 1)
                    v.tensor_tensor(wlb[p][:, 0:512], stg[p][:, 0:512],
                                    whb[p][:, 0:512], op=ALU.subtract) \
                        .then_inc(S["s_l"], 1)

            @blk.tensor
            def _(t):
                t.wait_ge(S["s_l"], bl + 2)
                for nt in range(2):
                    for kc in range(32):
                        p, c = kc // 16, (kc % 16) * 32
                        wh = whb[p][:, c:c + 32]
                        wl = wlb[p][:, c:c + 32]
                        rh, rl = rhs_pair(kc, nt)
                        t.matmul(ps[nt][0:32, :], wh, rh, start=(kc == 0),
                                 stop=False)
                        t.matmul(ps[nt][0:32, :], wh, rl, start=False,
                                 stop=False)
                        mm = t.matmul(ps[nt][0:32, :], wl, rh, start=False,
                                      stop=(kc == 31))
                    mm.then_inc(S["s_mm"], 1)
                t.matmul(ps[4][0:1, 0:1], ones_col[0:1, :], ones_col[0:1, :],
                         start=True, stop=True).then_inc(S["s_pw"], 2)

            @blk.vector
            def _(v):
                v.wait_ge(S["s_mm"], bmm + 2)
                for nt in range(2):
                    v.tensor_copy(holdG[0:32, nt * 512:(nt + 1) * 512],
                                  ps[nt][0:32, :]).then_inc(S["s_v"], 1)
        cnt["s_w0"] += 16
        cnt["s_w1"] += 16
        cnt["s_h"] += 2
        cnt["s_l"] += 2
        cnt["s_pw"] += 2
        cnt["s_mm"] += 2
        cnt["s_v"] += 2

    def p_attention():
        units = [(h, qb) for h in range(32) for qb in range(2)]
        nks = [16 if qb == 0 else 32 for (h, qb) in units]
        ngs = [nk // 8 for nk in nks]
        cum_mm = np.cumsum([0] + [2 * nk + 4 for nk in nks])
        cum_a = np.cumsum([0] + [nk + 1 for nk in nks])
        cum_v = np.cumsum([0] + [2 * nk + 2 for nk in nks])
        ndma = [nk // 4 + 3 for nk in nks]
        bmm, ba, bv, bcc = (cnt["s_mm"], cnt["s_a"], cnt["s_v"], cnt["s_cc"])
        bxp = [cnt["s_x0"], cnt["s_x1"]]
        bstp = [cnt["s_st0"], cnt["s_st1"]]
        bmk = [cnt["s_mk0"], cnt["s_mk1"]]
        bw_a = [cnt["s_w0"], cnt["s_w1"]]
        bh_a, bl_a = cnt["s_h"], cnt["s_l"]
        cum_dma_p = [[0], [0]]
        cum_vdma_p = [[0], [0]]
        for u, nk in enumerate(nks):
            cum_dma_p[u % 2].append(cum_dma_p[u % 2][-1] + ndma[u])
            cum_vdma_p[u % 2].append(cum_vdma_p[u % 2][-1] + nk)

        def mask_pos(g):
            return 1 if g == 0 else 2 * g

        def lo_pos(g, nk):
            return 2 * g + 3 if g <= nk - 2 else 2 * nk
        mk_ord = []
        tot_mk = [0, 0]
        gidx = 0
        for u, nk in enumerate(nks):
            row = []
            for tgi in range(ngs[u]):
                par = gidx % 2
                tot_mk[par] += 1
                row.append((par, tot_mk[par]))
                gidx += 1
            mk_ord.append(row)
        Kb = [arena[:, 0:4096], arena[:, 4096:8192]]
        Vhb = [abf[:, 16384:20480], abf[:, 24576:28672]]
        Vlb = [abf[:, 20480:24576], abf[:, 28672:32768]]
        vstg = arena[:, 27136:31232]
        pth = [abf[:, 52224:52736], abf[:, 53248:53760]]
        ptl = [abf[:, 52736:53248], abf[:, 53760:54272]]
        qbf = [arena[:, 16384:16896], arena[:, 16896:17408]]
        PT = [arena[:, 17408:17920], arena[:, 17920:18432]]
        rlb_sb = arena[:, 18432:18944]
        rlab_sb = arena[:, 18944:19456]
        t2_sb = arena[:, 19456:19968]
        Pa = arena[0:10, 19968:20480]
        akb = [arena[:, 20480:20496], arena[:, 20496:20512]]
        avb = [arena[0:10, 20512:20640], arena[0:10, 20640:20768]]
        lrow = arena[0:1, 20768:21280]
        larow = arena[0:1, 21280:21792]
        mkb = [arena[:, 22016 + i * 2048:22016 + (i + 1) * 2048].bitcast(BF16)
               for i in range(2)]
        gmap = []
        for u in range(len(units)):
            for tgi in range(ngs[u]):
                gmap.append((u, tgi))

        def pe_pos(nk):
            pos = {}
            p = 0
            for key in [("ST", 0), ("ST", 1), "sa"]:
                p += 1
                pos[key] = p
            for g in range(nk):
                if g + 2 <= nk - 1:
                    p += 1
                    pos[("ST", g + 2)] = p
                p += 1
                pos[("PV", g)] = p
            for key in ["la", "rlb", "rlab"]:
                p += 1
                pos[key] = p
            return pos

        with nc.Block() as blk:
            @blk.sync
            def _(e):
                e.wait_ge(S["s_cc"], bcc)
                gi = 0
                for u, (h, qb) in enumerate(units):
                    nk = nks[u]
                    if u >= 2:
                        e.wait_ge(S["s_mm"], bmm + int(cum_mm[u - 1]))
                    for R in range(nk // 4):
                        r, sl = _run_owner(R)
                        e.dma_start(Kb[u % 2][:, R * 512:(R + 1) * 512],
                                    k_all[r, h][:, sl * 512:(sl + 1) * 512]) \
                            .then_inc(SX(u), 16)
                    e.dma_start(qbf[u % 2],
                                st_qa[h][:, qb * 512:(qb + 1) * 512]) \
                        .then_inc(SX(u), 16)
                    if u >= 1:
                        e.wait_ge(S["s_l"], bl_a + u)
                    for R in range(nk // 4):
                        r, sl = _run_owner(R)
                        for m in range(4):
                            e.dma_start(
                                vstg[:, R * 512 + m * 128:
                                     R * 512 + (m + 1) * 128],
                                v_all[r, h, sl * 4 + m]) \
                                .then_inc(S["s_w%d" % (u % 2)], 16)
                    e.dma_start(akb[u % 2], st_akm[h]).then_inc(SX(u), 16)
                    e.dma_start(avb[u % 2], st_avk[h]).then_inc(SX(u), 16)
                    for tgi in range(ngs[u]):
                        par = gi % 2
                        if gi >= 2:
                            pu, pt = gmap[gi - 2]
                            e.wait_ge(S["s_v"],
                                      bv + int(cum_v[pu]) + 16 * pt + 14)
                        e.dma_start(
                            mkb[par].rearrange("p (c n) -> p c n", c=8),
                            mask_tbl[qb, tgi * 8:(tgi + 1) * 8]
                            .transpose([1, 0, 2])) \
                            .then_inc(S["s_mk%d" % par], 16)
                        gi += 1

            @blk.tensor
            def _(t):
                for u, (h, qb) in enumerate(units):
                    nk = nks[u]
                    mmc = bmm + int(cum_mm[u])
                    pos = pe_pos(nk)

                    def st(g):
                        t.matmul(ps[g % 2],
                                 Kb[u % 2][:, g * 128:(g + 1) * 128],
                                 qbf[u % 2], start=True, stop=True) \
                            .then_inc(S["s_mm"], 1)
                    t.wait_ge(SX(u), bxp[u % 2] + 16 * cum_dma_p[u % 2][u // 2 + 1])
                    if u >= 1:
                        t.wait_ge(S["s_v"], bv + int(cum_v[u]))
                    st(0)
                    st(1)
                    t.matmul(ps[3][0:10, :], akb[u % 2][:, 0:10], qbf[u % 2],
                             start=True, stop=True).then_inc(S["s_mm"], 1)
                    t.wait_ge(S["s_h"], bh_a + u + 1)
                    t.wait_ge(S["s_l"], bl_a + u + 1)
                    for g in range(nk):
                        t.wait_ge(S["s_a"], ba + int(cum_a[u]) + g + 1)
                        if g + 2 <= nk - 1:
                            st(g + 2)
                        t.wait_ge(S["s_v"], bv + int(cum_v[u]) + lo_pos(g, nk))
                        vt = slice(g * 128, (g + 1) * 128)
                        t.matmul(ps[4], Vhb[u % 2][:, vt], pth[g % 2],
                                 start=(g == 0), stop=False)
                        t.matmul(ps[4], Vhb[u % 2][:, vt], ptl[g % 2],
                                 start=False, stop=False)
                        t.matmul(ps[4], Vlb[u % 2][:, vt], pth[g % 2],
                                 start=False, stop=(g == nk - 1))
                        t.matmul(ps[5][0:1, :], onescb, pth[g % 2],
                                 start=(g == 0), stop=False)
                        t.matmul(ps[5][0:1, :], onescb, ptl[g % 2],
                                 start=False, stop=(g == nk - 1)) \
                            .then_inc(S["s_mm"], 1)
                    t.wait_ge(S["s_a"], ba + int(cum_a[u]) + nk + 1)
                    t.matmul(ps[6], avb[u % 2], Pa, start=True, stop=True)
                    t.matmul(ps[7][0:1, :], ones_col[0:10, :], Pa,
                             start=True, stop=True).then_inc(S["s_mm"], 1)
                    t.wait_ge(S["s_v"], bv + int(cum_v[u]) + 2 * nk + 1)
                    t.matmul(ps[0], ones_row, lrow, start=True, stop=True) \
                        .then_inc(S["s_mm"], 1)
                    t.matmul(ps[1], ones_row, larow, start=True, stop=True) \
                        .then_inc(S["s_mm"], 1)

            @blk.scalar
            def _(s):
                for u, (h, qb) in enumerate(units):
                    nk = nks[u]
                    mmc = bmm + int(cum_mm[u])
                    pos = pe_pos(nk)
                    s.wait_ge(S["s_w%d" % (u % 2)],
                              bw_a[u % 2] + 16 * cum_vdma_p[u % 2][u // 2 + 1])
                    if u >= 2:
                        s.wait_ge(S["s_mm"], bmm + int(cum_mm[u - 1]))
                    s.activation(Vhb[u % 2][:, 0:nk * 128],
                                 vstg[:, 0:nk * 128], AF.Copy) \
                        .then_inc(S["s_h"], 1)
                    for g in range(nk):
                        s.wait_ge(S["s_v"], bv + int(cum_v[u]) + mask_pos(g))
                        if g < 2:
                            s.wait_ge(S["s_mm"], bmm + int(cum_mm[u]))
                        else:
                            s.wait_ge(S["s_mm"], mmc + pos[("PV", g - 2)])
                        s.activation(PT[g % 2], ps[g % 2], AF.Exp,
                                     bias=col_m30)
                        s.activation(pth[g % 2], PT[g % 2], AF.Copy) \
                            .then_inc(S["s_a"], 1)
                    s.wait_ge(S["s_mm"], mmc + pos["sa"])
                    s.activation(Pa, ps[3][0:10, :], AF.Exp,
                                 bias=col_m10[0:10, :]).then_inc(S["s_a"], 1)
                    s.wait_ge(S["s_v"], bv + int(cum_v[u + 1]))
                    s.dma_start(st_ya[h][:, qb * 512:(qb + 1) * 512],
                                obuf[:, (u % 2) * 512:(u % 2 + 1) * 512]) \
                        .then_inc(SST(u % 2), 16)

            @blk.vector
            def _(v):
                def pt_lo(u, g):
                    v.wait_ge(S["s_a"], ba + int(cum_a[u]) + g + 1)
                    v.tensor_tensor(ptl[g % 2], PT[g % 2], pth[g % 2],
                                    op=ALU.subtract).then_inc(S["s_v"], 1)
                for u, (h, qb) in enumerate(units):
                    nk = nks[u]
                    mmc = bmm + int(cum_mm[u])
                    pos = pe_pos(nk)
                    v.wait_ge(S["s_h"], bh_a + u + 1)
                    if u >= 2:
                        v.wait_ge(S["s_mm"], bmm + int(cum_mm[u - 1]))
                    v.tensor_tensor(Vlb[u % 2][:, 0:nk * 128],
                                    vstg[:, 0:nk * 128],
                                    Vhb[u % 2][:, 0:nk * 128],
                                    op=ALU.subtract).then_inc(S["s_l"], 1)
                    for g in range(nk):
                        v.wait_ge(S["s_mm"], mmc + pos[("ST", g)])
                        par, cnt_mk = mk_ord[u][g // 8]
                        v.wait_ge(S["s_mk%d" % par], bmk[par] + 16 * cnt_mk)
                        v.tensor_tensor(
                            ps[g % 2], ps[g % 2],
                            mkb[par][:, (g % 8) * 512:(g % 8 + 1) * 512],
                            op=ALU.add).then_inc(S["s_v"], 1)
                        if g >= 1:
                            pt_lo(u, g - 1)
                    pt_lo(u, nk - 1)
                    v.wait_ge(S["s_mm"], mmc + pos["la"])
                    v.tensor_copy(lrow, ps[5][0:1, :])
                    v.reciprocal(lrow, lrow)
                    v.tensor_copy(larow, ps[7][0:1, :])
                    v.reciprocal(larow, larow).then_inc(S["s_v"], 1)
                    ob = obuf[:, (u % 2) * 512:(u % 2 + 1) * 512]
                    if u >= 2:
                        v.wait_ge(SST(u % 2), bstp[u % 2] + 16 * (u // 2))
                    v.wait_ge(S["s_mm"], mmc + pos["rlb"])
                    v.tensor_copy(rlb_sb, ps[0])
                    v.tensor_tensor(ob, ps[4], rlb_sb, op=ALU.mult)
                    v.wait_ge(S["s_mm"], mmc + pos["rlab"])
                    v.tensor_copy(rlab_sb, ps[1])
                    v.tensor_tensor(t2_sb, ps[6], rlab_sb, op=ALU.mult)
                    v.scalar_tensor_tensor(ob, t2_sb, col_gat, ob,
                                           op0=ALU.mult, op1=ALU.add) \
                        .then_inc(S["s_v"], 1)

        cnt["s_mm"] += int(cum_mm[-1])
        cnt["s_a"] += int(cum_a[-1])
        cnt["s_v"] += int(cum_v[-1])
        for u in range(len(units)):
            cnt["s_x%d" % (u % 2)] += 16 * ndma[u]
        cnt["s_w0"] += 16 * cum_vdma_p[0][-1]
        cnt["s_w1"] += 16 * cum_vdma_p[1][-1]
        cnt["s_h"] += len(units)
        cnt["s_l"] += len(units)
        cnt["s_mk0"] += 16 * tot_mk[0]
        cnt["s_mk1"] += 16 * tot_mk[1]
        cnt["s_st0"] += 16 * 32
        cnt["s_st1"] += 16 * 32

    def p_kkm():
        bx, bmm, bv = cnt["s_x"], cnt["s_mm"], cnt["s_v"]
        bst4 = [cnt["s_st%d" % i] for i in range(4)]
        with nc.Block() as blk:
            @blk.sync
            def _(e):
                for kv in range(8):
                    e.dma_start(arena[:, 8192 + kv * NT:8192 + (kv + 1) * NT],
                                st_kfm[kv]).then_inc(S["s_x"], 16)

            @blk.tensor
            def _(t):
                t.wait_ge(S["s_x"], bx + 16 * 8)
                for u in range(64):
                    kv, c = u // 8, u % 8
                    if u >= 2:
                        t.wait_ge(S["s_v"], bv + u - 1)
                    t.transpose(ps[4 + u % 2][:, 0:128],
                                arena[:, 8192 + kv * NT + c * 128:
                                      8192 + kv * NT + (c + 1) * 128],
                                ident).then_inc(S["s_mm"], 1)

            @blk.vector
            def _(v):
                for u in range(64):
                    v.wait_ge(S["s_mm"], bmm + u + 1)
                    if u >= 4:
                        v.wait_ge(SST(u % 4), bst4[u % 4] + 16 * (u // 4))
                    v.tensor_copy(obuf[:, (u % 4) * 512:(u % 4) * 512 + 128],
                                  ps[4 + u % 2][:, 0:128]).then_inc(S["s_v"], 1)

            @blk.scalar
            def _(s):
                for u in range(64):
                    kv, c = u // 8, u % 8
                    s.wait_ge(S["s_v"], bv + u + 1)
                    s.dma_start(st_kkm[kv, c],
                                obuf[:, (u % 4) * 512:(u % 4) * 512 + 128]) \
                        .then_inc(SST(u % 4), 16)
        cnt["s_x"] += 16 * 8
        cnt["s_mm"] += 64
        cnt["s_v"] += 64
        for i in range(4):
            cnt["s_st%d" % i] += 16 * 16

    def p_prep():
        ba, bv, bmm = cnt["s_a"], cnt["s_v"], cnt["s_mm"]
        dtr = holdG[0:32, 0:NT]
        gr = holdG[32:64, 0:NT]
        Gr = holdG[64:96, 0:NT]
        t_ln = misc[0:32, 0:NT]
        t_relu = misc[0:32, NT:2 * NT]
        with nc.Block() as blk:
            @blk.vector
            def _(v):
                v.tensor_scalar(dtr, dtr, col_dtb, None, op0=ALU.add) \
                    .then_inc(S["s_v"], 1)
                v.wait_ge(S["s_a"], ba + 3)
                v.tensor_tensor(dtr, t_ln, t_relu, op=ALU.add) \
                    .then_inc(S["s_v"], 1)
                v.tensor_scalar(gr, dtr, col_alog, None, op0=ALU.mult) \
                    .then_inc(S["s_v"], 1)
                v.reduce_sum(small[0:32, 10:11], gr[:, 0:512],
                             axis=mybir.AxisListType.X)
                v.reduce_sum(small[0:32, 11:12], gr[:, 512:1024],
                             axis=mybir.AxisListType.X).then_inc(S["s_v"], 1)
                for c in range(8):
                    v.tensor_tensor_scan(Gr[:, c * 128:(c + 1) * 128],
                                         gr[:, c * 128:(c + 1) * 128],
                                         gr[:, c * 128:(c + 1) * 128],
                                         zcol, op0=ALU.add, op1=ALU.bypass)
                v.memset(small[0:1, 12:13], 0.0).then_inc(S["s_v"], 1)

            @blk.scalar
            def _(s):
                s.wait_ge(S["s_v"], bv + 1)
                s.activation(t_ln, dtr, AF.Abs)
                s.activation(t_ln, t_ln, AF.Exp, scale=-1.0)
                s.activation(t_ln, t_ln, AF.Ln, bias=1.0).then_inc(S["s_a"], 1)
                s.activation(t_relu, dtr, AF.Relu).then_inc(S["s_a"], 2)
                s.wait_ge(S["s_v"], bv + 4)
                s.activation(small[0:32, 10:12], small[0:32, 10:12], AF.Exp) \
                    .then_inc(S["s_a"], 1)
                s.dma_start(ctr_d.transpose([1, 0]), small[0:32, 10:12],
                            allow_slow_non_contiguous=True) \
                    .then_inc(S["s_st0"], 16)

            @blk.tensor
            def _(t):
                t.wait_ge(S["s_v"], bv + 5)
                for c in range(8):
                    t.transpose(ps[4 + c % 2][0:128, 0:32],
                                Gr[:, c * 128:(c + 1) * 128],
                                ident32s[64:96, :]).then_inc(S["s_mm"], 1)
                    t.wait_ge(S["s_v"], bv + 5 + c + 1)
                for c in range(8):
                    t.transpose(ps[6 + c % 2][0:128, 0:32],
                                dtr[:, c * 128:(c + 1) * 128],
                                ident32s[0:32, :]).then_inc(S["s_mm"], 1)
                    t.wait_ge(S["s_v"], bv + 13 + c + 1)

            @blk.vector
            def _(v):
                for c in range(8):
                    v.wait_ge(S["s_mm"], bmm + c + 1)
                    v.tensor_copy(holdT[:, 256 + c * 32:256 + (c + 1) * 32],
                                  ps[4 + c % 2][:, 0:32]).then_inc(S["s_v"], 1)
                for c in range(8):
                    v.wait_ge(S["s_mm"], bmm + 8 + c + 1)
                    v.tensor_copy(holdT[:, c * 32:(c + 1) * 32],
                                  ps[6 + c % 2][:, 0:32]).then_inc(S["s_v"], 1)
        cnt["s_v"] += 5 + 16
        cnt["s_a"] += 4
        cnt["s_mm"] += 16
        cnt["s_st0"] += 16

    def p_sweep1():
        snapA = lambda h: arena[:, 2048 + h * 128:2048 + (h + 1) * 128]
        snapB = lambda h: arena[:, 6144 + h * 128:6144 + (h + 1) * 128]
        stw = lambda h: arena[:, 10240 + h * 128:10240 + (h + 1) * 128]
        colb = [arena[:, 14336 + i * 128:14336 + (i + 1) * 128] for i in range(2)]
        kkmb = [arena[:, 14592 + i * 128:14592 + (i + 1) * 128] for i in range(2)]
        vkmb = [arena[:, 14848 + i * 128:14848 + (i + 1) * 128] for i in range(2)]
        xcb = [arena[:, 15104 + i * 128:15104 + (i + 1) * 128] for i in range(2)]
        kdb = [arena[:, 15360 + i * 128:15360 + (i + 1) * 128] for i in range(2)]
        wcol = [arena[:, 15616 + i:15616 + i + 1] for i in range(2)]
        cdcol = [arena[:, 15620 + i:15620 + i + 1] for i in range(2)]
        units = [(h, r, c) for h in range(32) for r in range(2)
                 for c in range(4)]
        bmm, bv, ba = cnt["s_mm"], cnt["s_v"], cnt["s_a"]
        bxp = [cnt["s_x0"], cnt["s_x1"]]
        bst = cnt["s_st0"]
        with nc.Block() as blk:
            @blk.sync
            def _(e):
                for u, (h, r, c) in enumerate(units):
                    if u >= 2:
                        e.wait_ge(S["s_mm"], bmm + 2 * (u - 1))
                    kv, cl = h // 4, r * 4 + c
                    e.dma_start(kkmb[u % 2], st_kkm[kv, cl]).then_inc(SX(u), 16)
                    e.dma_start(vkmb[u % 2], st_vkm[kv, cl]).then_inc(SX(u), 16)

            @blk.tensor
            def _(t):
                for u, (h, r, c) in enumerate(units):
                    t.wait_ge(S["s_v"], bv + 4 * u + 1)
                    if u >= 2:
                        t.wait_ge(S["s_a"], ba + (u - 2) + 1)
                    t.transpose(ps[0 + u % 2][:, 0:128], colb[u % 2], ident) \
                        .then_inc(S["s_mm"], 1)
                    t.wait_ge(S["s_v"], bv + 4 * u + 3)
                    t.matmul(ps[2 + u % 2][:, 0:128], kdb[u % 2], xcb[u % 2],
                             start=True, stop=True).then_inc(S["s_mm"], 1)

            @blk.scalar
            def _(s):
                bg0 = cnt["s_g"]
                for u, (h, r, c) in enumerate(units):
                    s.wait_ge(S["s_v"], bv + 4 * u + 2)
                    if u >= 2:
                        s.wait_ge(S["s_v"], bv + 4 * (u - 2) + 4)
                    s.activation(wcol[u % 2], misc[:, 1024 + (u % 2):
                                                   1025 + (u % 2)], AF.Exp)
                    s.activation(cdcol[u % 2],
                                 ps[0 + u % 2][:, 127:128], AF.Exp) \
                        .then_inc(S["s_a"], 1)
                    if c == 3:
                        idx = (u + 1) // 4
                        s.wait_ge(S["s_g"], bg0 + idx)
                        s.dma_start(ctr_s[r, h], obuf[:, 1536:1664]) \
                            .then_inc(S["s_st0"], 16)

            @blk.vector
            def _(v):
                for u, (h, r, c) in enumerate(units):
                    cl = r * 4 + c
                    gcol = holdT[:, 256 + cl * 32 + h:256 + cl * 32 + h + 1]
                    dcol = holdT[:, cl * 32 + h:cl * 32 + h + 1]
                    v.tensor_scalar(colb[u % 2], ones128, gcol, None,
                                    op0=ALU.mult).then_inc(S["s_v"], 1)
                    v.wait_ge(S["s_mm"], bmm + 2 * u + 1)
                    v.tensor_scalar(misc[:, 1024 + (u % 2):1025 + (u % 2)],
                                    ps[0 + u % 2][:, 127:128], gcol, None,
                                    op0=ALU.subtract).then_inc(S["s_v"], 1)
                    v.wait_ge(SX(u), bxp[u % 2] + 16 * 2 * (u // 2 + 1))
                    v.wait_ge(S["s_a"], ba + u + 1)
                    v.tensor_scalar(xcb[u % 2], vkmb[u % 2], dcol, None,
                                    op0=ALU.mult)
                    v.tensor_scalar(kdb[u % 2], kkmb[u % 2], wcol[u % 2], None,
                                    op0=ALU.mult).then_inc(S["s_v"], 1)
                    v.wait_ge(S["s_mm"], bmm + 2 * u + 2)
                    if c == 0:
                        v.tensor_copy(stw(h), ps[2 + u % 2][:, 0:128]) \
                            .then_inc(S["s_v"], 1)
                    else:
                        v.scalar_tensor_tensor(stw(h), stw(h), cdcol[u % 2],
                                               ps[2 + u % 2][:, 0:128],
                                               op0=ALU.mult, op1=ALU.add) \
                            .then_inc(S["s_v"], 1)
                    if c == 3:
                        k = (u + 1) // 4
                        if k >= 2:
                            v.wait_ge(S["s_st0"], bst + 16 * (k - 1))
                        v.tensor_copy(obuf[:, 1536:1664], stw(h)) \
                            .then_inc(S["s_g"], 1)


        cnt["s_x0"] += 16 * len(units)
        cnt["s_x1"] += 16 * len(units)
        cnt["s_mm"] += 2 * len(units)
        cnt["s_v"] += 4 * len(units)
        cnt["s_a"] += len(units)
        cnt["s_g"] += 64
        cnt["s_st0"] += 16 * 64

        tots1 = tot_stores()
        bcc = cnt["s_cc"]
        with nc.Block() as blk:
            @blk.gpsimd
            def _(g):
                for sem, vv in tots1:
                    g.wait_ge(sem, vv)
                g.collective_compute("AllGather", ALU.bypass,
                                     replica_groups=RG, ins=[ctr_s.opt()],
                                     outs=[s_all.opt()]).then_inc(S["s_cc"], 1)
                g.collective_compute("AllGather", ALU.bypass,
                                     replica_groups=RG, ins=[ctr_d.opt()],
                                     outs=[d_all.opt()]).then_inc(S["s_cc"], 1)
        cnt["s_cc"] += 2

    def p_prefix():
        snapA = lambda h: arena[:, 2048 + h * 128:2048 + (h + 1) * 128]
        snapB = lambda h: arena[:, 6144 + h * 128:6144 + (h + 1) * 128]
        Tz = lambda g: arena[:, 10240 + g * 128:10240 + (g + 1) * 128]
        sbuf_s = [arena[:, 11264 + i * 128:11264 + (i + 1) * 128]
                  for i in range(2)]
        rowD = misc[0:1, 0:256]
        Dbc = arena[:, 11520:11776]
        bx, bmm, bv, bcc = cnt["s_x"], cnt["s_mm"], cnt["s_v"], cnt["s_cc"]
        bd = [cnt["s_d0"], cnt["s_d1"]]
        with nc.Block() as blk:
            @blk.sync
            def _(e):
                e.wait_ge(S["s_cc"], bcc)
                for g in range(8):
                    r, sl = _run_owner(g)
                    e.dma_start(rowD[0:1, g * 32:(g + 1) * 32], d_all[r, sl]) \
                        .then_inc(S["s_x"], 16)
                for h in range(32):
                    for g in range(7):
                        k = h * 7 + g
                        r, sl = _run_owner(g)
                        if k >= 2:
                            e.wait_ge(S["s_v"], bv + 1 + k - 1)
                        e.dma_start(sbuf_s[k % 2], s_all[r, sl, h]) \
                            .then_inc(S["s_d%d" % (k % 2)], 16)

            @blk.tensor
            def _(t):
                t.wait_ge(S["s_x"], bx + 16 * 8)
                t.matmul(ps[0][:, 0:256], ones_row, rowD,
                         start=True, stop=True).then_inc(S["s_mm"], 1)

            @blk.vector
            def _(v):
                v.wait_ge(S["s_mm"], bmm + 1)
                v.tensor_copy(Dbc, ps[0][:, 0:256]).then_inc(S["s_v"], 1)
                for h in range(32):
                    for g in range(7):
                        k = h * 7 + g
                        v.wait_ge(S["s_d%d" % (k % 2)],
                                  bd[k % 2] + 16 * (k // 2 + 1))
                        if g == 0:
                            v.tensor_copy(Tz(0), sbuf_s[k % 2])
                        else:
                            v.scalar_tensor_tensor(
                                Tz(g), Tz(g - 1), Dbc[:, g * 32 + h:
                                                      g * 32 + h + 1],
                                sbuf_s[k % 2], op0=ALU.mult, op1=ALU.add)
                        v.memset(small[0:1, 13:14], 0.0) \
                            .then_inc(S["s_v"], 1)
                    v.memset(snapA(h), 0.0)
                    v.memset(snapB(h), 0.0)
                    for g in range(7):
                        v.scalar_tensor_tensor(snapA(h), Tz(g),
                                               selbc[:, g:g + 1], snapA(h),
                                               op0=ALU.mult, op1=ALU.add)
                        v.scalar_tensor_tensor(snapB(h), Tz(g),
                                               selbc[:, 8 + g:9 + g], snapB(h),
                                               op0=ALU.mult, op1=ALU.add)
        cnt["s_x"] += 16 * 8
        cnt["s_d0"] += 16 * 112
        cnt["s_d1"] += 16 * 112
        cnt["s_mm"] += 1
        cnt["s_v"] += 1 + 224

    def p_sweep2():
        snapA = lambda h: arena[:, 2048 + h * 128:2048 + (h + 1) * 128]
        snapB = lambda h: arena[:, 6144 + h * 128:6144 + (h + 1) * 128]
        Z = 12288
        BW = 2048
        colb = [arena[:, Z + i * BW:Z + i * BW + 128] for i in range(2)]
        dcT = [arena[:, Z + i * BW + 128:Z + i * BW + 256] for i in range(2)]
        exG = [arena[:, Z + i * BW + 256:Z + i * BW + 384] for i in range(2)]
        SDT = [arena[:, Z + i * BW + 384:Z + i * BW + 512] for i in range(2)]
        xcb = [arena[:, Z + i * BW + 512:Z + i * BW + 640] for i in range(2)]
        kdb = [arena[:, Z + i * BW + 640:Z + i * BW + 768] for i in range(2)]
        qsc = [arena[:, Z + i * BW + 768:Z + i * BW + 896] for i in range(2)]
        ysq = [arena[:, Z + i * BW + 896:Z + i * BW + 1024] for i in range(2)]
        rbs = [arena[:, Z + i * BW + 1024:Z + i * BW + 1152] for i in range(2)]
        ynb = [arena[:, Z + i * BW + 1152:Z + i * BW + 1280] for i in range(2)]
        gsb = [arena[:, Z + i * BW + 1280:Z + i * BW + 1408] for i in range(2)]
        rro = [arena[0:1, Z + i * BW + 1412:Z + i * BW + 1540]
               for i in range(2)]
        Z2 = Z + 2 * BW
        qb_ = [arena[:, Z2 + i * 128:Z2 + (i + 1) * 128] for i in range(2)]
        gt_ = [arena[:, Z2 + 256 + i * 128:Z2 + 256 + (i + 1) * 128]
               for i in range(2)]
        kf_ = [arena[:, Z2 + 512 + i * 128:Z2 + 512 + (i + 1) * 128]
               for i in range(2)]
        kk_ = [arena[:, Z2 + 768 + i * 128:Z2 + 768 + (i + 1) * 128]
               for i in range(2)]
        vk_ = [arena[:, Z2 + 1024 + i * 128:Z2 + 1024 + (i + 1) * 128]
               for i in range(2)]
        units = [(h, r, c) for h in range(32) for r in range(2)
                 for c in range(4)]
        bmm, bv, ba = cnt["s_mm"], cnt["s_v"], cnt["s_a"]
        bxp = [cnt["s_x0"], cnt["s_x1"]]
        bst4 = [cnt["s_st%d" % i] for i in range(4)]
        with nc.Block() as blk:
            @blk.sync
            def _(e):
                for u, (h, r, c) in enumerate(units):
                    if u >= 2:
                        e.wait_ge(S["s_mm"], bmm + 6 * (u - 1))
                    kv, cl = h // 4, r * 4 + c
                    tsl = slice(cl * 128, (cl + 1) * 128)
                    e.dma_start(qb_[u % 2], st_qb[h][:, tsl]).then_inc(SX(u), 16)
                    e.dma_start(gt_[u % 2], st_gate[h][:, tsl]).then_inc(SX(u), 16)
                    e.dma_start(kf_[u % 2], st_kfm[kv][:, tsl]).then_inc(SX(u), 16)
                    e.dma_start(kk_[u % 2], st_kkm[kv, cl]).then_inc(SX(u), 16)
                    e.dma_start(vk_[u % 2], st_vkm[kv, cl]).then_inc(SX(u), 16)

            @blk.tensor
            def _(t):
                for u, (h, r, c) in enumerate(units):
                    m0 = bmm + 6 * u
                    v0 = bv + 6 * u
                    a0 = ba + 4 * u
                    t.wait_ge(S["s_v"], v0 + 1)
                    if u >= 1:
                        t.wait_ge(S["s_a"], ba + 4 * (u - 1) + 4)
                    t.transpose(ps[0][:, 0:128], colb[u % 2], ident) \
                        .then_inc(S["s_mm"], 1)
                    t.wait_ge(SX(u), bxp[u % 2] + 16 * 5 * (u // 2 + 1))
                    t.matmul(ps[1][:, 0:128], kf_[u % 2], qb_[u % 2],
                             start=True, stop=True).then_inc(S["s_mm"], 1)
                    t.wait_ge(S["s_v"], v0 + 3)
                    st_tile = snapA(h) if r == 0 else snapB(h)
                    t.matmul(ps[2][:, 0:128], xcb[u % 2], SDT[u % 2],
                             start=True, stop=False)
                    t.matmul(ps[2][:, 0:128], st_tile, qsc[u % 2],
                             start=False, stop=True).then_inc(S["s_mm"], 1)
                    t.matmul(ps[3][:, 0:128], kdb[u % 2], xcb[u % 2],
                             start=True, stop=True).then_inc(S["s_mm"], 1)
                    t.wait_ge(S["s_a"], a0 + 2)
                    t.matmul(ps[4][0:1, 0:128], ones_col, ysq[u % 2],
                             start=True, stop=True).then_inc(S["s_mm"], 1)
                    t.wait_ge(S["s_v"], v0 + 5)
                    t.matmul(ps[5][:, 0:128], ones_row, rro[u % 2],
                             start=True, stop=True).then_inc(S["s_mm"], 1)

            @blk.scalar
            def _(s):
                for u, (h, r, c) in enumerate(units):
                    m0 = bmm + 6 * u
                    v0 = bv + 6 * u
                    a0 = ba + 4 * u
                    s.wait_ge(S["s_v"], v0 + 2)
                    if u >= 2:
                        s.wait_ge(S["s_v"], bv + 6 * (u - 2) + 3)
                    s.activation(dcT[u % 2], misc[:, 1152 + (u % 2) * 128:
                                                   1152 + (u % 2 + 1) * 128],
                                 AF.Exp)
                    s.activation(exG[u % 2], ps[0][:, 0:128], AF.Exp)
                    s.activation(misc[:, 1024 + (u % 2):1025 + (u % 2)],
                                 misc[:, 1026 + (u % 2):1027 + (u % 2)],
                                 AF.Exp).then_inc(S["s_a"], 1)
                    s.wait_ge(S["s_mm"], m0 + 3)
                    s.activation(ysq[u % 2], ps[2][:, 0:128], AF.Square) \
                        .then_inc(S["s_a"], 1)
                    s.wait_ge(S["s_mm"], m0 + 5)
                    s.activation(rro[u % 2], ps[4][0:1, 0:128], AF.Sqrt,
                                 scale=1.0 / 128.0, bias=col_eps[0:1, :]) \
                        .then_inc(S["s_a"], 1)
                    s.wait_ge(SX(u), bxp[u % 2] + 16 * 5 * (u // 2 + 1))
                    s.activation(gsb[u % 2], gt_[u % 2], AF.Silu) \
                        .then_inc(S["s_a"], 1)
                    s.wait_ge(S["s_v"], bv + 6 * (u + 1))
                    s.dma_start(st_yb[h][:, (r * 4 + c) * 128:
                                         (r * 4 + c + 1) * 128],
                                obuf[:, (u % 4) * 512:(u % 4) * 512 + 128]) \
                        .then_inc(SST(u % 4), 16)

            @blk.vector
            def _(v):
                for u, (h, r, c) in enumerate(units):
                    m0 = bmm + 6 * u
                    v0 = bv + 6 * u
                    a0 = ba + 4 * u
                    cl = r * 4 + c
                    gcol = holdT[:, 256 + cl * 32 + h:256 + cl * 32 + h + 1]
                    dcol = holdT[:, cl * 32 + h:cl * 32 + h + 1]
                    if u >= 4:
                        v.wait_ge(SST(u % 4), bst4[u % 4] + 16 * (u // 4))
                    v.tensor_scalar(colb[u % 2], ones128, gcol, None,
                                    op0=ALU.mult).then_inc(S["s_v"], 1)
                    v.wait_ge(S["s_mm"], m0 + 1)
                    v.scalar_tensor_tensor(
                        misc[:, 1152 + (u % 2) * 128:1152 + (u % 2 + 1) * 128],
                        ps[0][:, 0:128], gcol, dmaskT,
                        op0=ALU.subtract, op1=ALU.add)
                    v.tensor_scalar(misc[:, 1026 + (u % 2):1027 + (u % 2)],
                                    ps[0][:, 127:128], gcol, None,
                                    op0=ALU.subtract).then_inc(S["s_v"], 1)
                    v.wait_ge(S["s_mm"], m0 + 2)
                    v.wait_ge(S["s_a"], a0 + 1)
                    v.tensor_tensor(SDT[u % 2], ps[1][:, 0:128], dcT[u % 2],
                                    op=ALU.mult)
                    v.tensor_scalar(xcb[u % 2], vk_[u % 2], dcol, None,
                                    op0=ALU.mult)
                    v.tensor_scalar(kdb[u % 2], kk_[u % 2],
                                    misc[:, 1024 + (u % 2):1025 + (u % 2)],
                                    None, op0=ALU.mult)
                    v.tensor_tensor(qsc[u % 2], qb_[u % 2], exG[u % 2],
                                    op=ALU.mult).then_inc(S["s_v"], 1)
                    v.wait_ge(S["s_mm"], m0 + 4)
                    st_tile = snapA(h) if r == 0 else snapB(h)
                    v.scalar_tensor_tensor(st_tile, st_tile,
                                           exG[u % 2][:, 127:128],
                                           ps[3][:, 0:128],
                                           op0=ALU.mult, op1=ALU.add) \
                        .then_inc(S["s_v"], 1)
                    v.wait_ge(S["s_a"], a0 + 3)
                    v.reciprocal(rro[u % 2], rro[u % 2]).then_inc(S["s_v"], 1)
                    v.wait_ge(S["s_mm"], m0 + 6)
                    v.tensor_copy(rbs[u % 2], ps[5][:, 0:128])
                    v.scalar_tensor_tensor(ynb[u % 2], ps[2][:, 0:128],
                                           col_gn, rbs[u % 2],
                                           op0=ALU.mult, op1=ALU.mult)
                    v.wait_ge(S["s_a"], a0 + 4)
                    v.tensor_tensor(obuf[:, (u % 4) * 512:(u % 4) * 512 + 128],
                                    ynb[u % 2], gsb[u % 2], op=ALU.mult) \
                        .then_inc(S["s_v"], 1)

        cnt["s_x0"] += 16 * 5 * (len(units) // 2)
        cnt["s_x1"] += 16 * 5 * (len(units) // 2)
        cnt["s_mm"] += 6 * len(units)
        cnt["s_a"] += 4 * len(units)
        cnt["s_v"] += 6 * len(units)
        for i in range(4):
            cnt["s_st%d" % i] += 16 * (len(units) // 4)

    if on():
      with nc.named_scope("ln0"):
        p_ln(x_fm, tap=("st_ln" in taps))
    if on():
      with nc.named_scope("proj_ka"):
        p_proj(wqk, 32, "rope", dstd=ctr_k, tabpair=(tabs["ck"], tabs["sk"]),
               adapter="ak", oc0=32)
    if on():
      with nc.named_scope("proj_va"):
        p_proj(wv_a, 32, "vtp", dstd=ctr_v, adapter="av")
    if on():
      with nc.named_scope("proj_qa"):
        tots0 = tot_stores()
        p_proj(wqk, 32, "rope", dstd=st_qa, tabpair=(tabs["cq"], tabs["sq"]))
        with nc.Block() as blk:
            @blk.gpsimd
            def _(g):
                for sem, vv in tots0:
                    g.wait_ge(sem, vv)
                g.collective_compute("AllGather", ALU.bypass,
                                     replica_groups=RG, ins=[ctr_k.opt()],
                                     outs=[k_all.opt()]).then_inc(S["s_cc"], 1)
                g.collective_compute("AllGather", ALU.bypass,
                                     replica_groups=RG, ins=[ctr_v.opt()],
                                     outs=[v_all.opt()]).then_inc(S["s_cc"], 1)
        cnt["s_cc"] += 2
        barrier_stores()
    if on():
      with nc.named_scope("attention"):
        p_attention()
        barrier_stores()
    if on():
      with nc.named_scope("proj_o_a"):
        p_load_arena(st_ya)
        p_proj(wproj, 32, "plain", dstd=st_x1, resid=x_fm)
        barrier_stores()
    if on():
      with nc.named_scope("ln1"):
        p_ln(st_x1)
    if on():
      with nc.named_scope("ffn1"):
        p_proj(lora_a, 1, "hA")
        p_proj(ffn_w, 32, "plain", dstd=st_x2, resid=st_x1,
               bias=lambda oc: fbias[:, oc:oc + 1], lora=(lora_b, hA))
        barrier_stores()
    if on():
      with nc.named_scope("ln2"):
        p_ln(st_x2)
    if on():
      with nc.named_scope("proj_qb"):
        p_proj(wq_b, 32, "rope", dstd=st_qb, tabpair=(tabs["ck"], tabs["sk"]))
    if on():
      with nc.named_scope("proj_kb"):
        p_proj(wk_b, 8, "rope", dstd=st_kfm, tabpair=(tabs["ck"], tabs["sk"]))
    if on():
      with nc.named_scope("proj_vb"):
        p_proj(wv_b, 8, "vtp", dstd=st_vkm)
    if on():
      with nc.named_scope("proj_gb"):
        p_proj(wg_b, 32, "plain", dstd=st_gate)
    if on():
      with nc.named_scope("p_dt"):
        p_dt()
        barrier_stores()
    if on():
      with nc.named_scope("p_kkm"):
        p_kkm()
        barrier_stores()
    if on():
      with nc.named_scope("p_prep"):
        p_prep()
        if "st_hg" in taps:
            bst3 = cnt["s_st"]
            with nc.Block() as blk:
                @blk.scalar
                def _(s):
                    s.dma_start(st_hg, holdG).then_inc(S["s_st"], 16)
                    s.wait_ge(S["s_st"], bst3 + 16)
            cnt["s_st"] += 16
        barrier_stores()
    if on():
      with nc.named_scope("sweep1"):
        p_sweep1()
    if on():
      with nc.named_scope("prefix"):
        p_prefix()
    if on():
      with nc.named_scope("sweep2"):
        p_sweep2()
        barrier_stores()
    if on():
      with nc.named_scope("proj_o_b"):
        p_load_arena(st_yb)
        p_proj(wo_b, 32, "plain", dstd=st_x3, resid=st_x2)
        barrier_stores()
    if on():
      with nc.named_scope("ln3"):
        p_ln(st_x3)
    if on():
      with nc.named_scope("ffn2"):
        p_proj(lora_a, 1, "hA")
        p_proj(ffn_w, 32, "plain", dstd=out_fm, resid=st_x3,
               bias=lambda oc: fbias[:, oc:oc + 1], lora=(lora_b, hA))

    global LAST_CNT
    LAST_CNT = dict(cnt)
    tots_f = tot_stores()
    tot_cc = cnt["s_cc"]
    with nc.Block() as blk:
        @blk.gpsimd
        def _(g):
            for sem, vv in tots_f:
                g.wait_ge(sem, vv)
            g.wait_ge(S["s_cc"], tot_cc)

    return nc


_NC_CACHE = {}


def kernel(**inputs):
    percore = host_prep(inputs)
    key = "full"
    if key not in _NC_CACHE:
        _NC_CACHE[key] = build_program()
    nc = _NC_CACHE[key]
    res = run_bass_kernel_spmd(nc, percore, core_ids=list(range(N_CORES)))
    out = np.empty((2, 4096, 4096), np.float32)
    for c in range(N_CORES):
        b, j = c // 4, c % 4
        tok = _tok_idx(j)
        out[b, tok, :] = res.results[c]["out_fm"].reshape(4096, NT).T
    return out
